# version 08
# baseline (speedup 1.0000x reference)
import numpy as np

import concourse.bass as bass
import concourse.mybir as mybir
from concourse.bass_utils import run_bass_kernel_spmd

F32 = mybir.dt.float32
BF16 = mybir.dt.bfloat16
AF = mybir.ActivationFunctionType
ALU = mybir.AluOpType
AX = mybir.AxisListType.X

NCORE = 8
D = 2048
T = 1024
DC = 16
NCH = 8
H = 16
NE = 8
FC = 11
FSC = 22
EPS = 1e-5
BIG = 1e30


class _Sem:
    __slots__ = ("h", "v")

    def __init__(self, h):
        self.h = h
        self.v = 0


class K:

    def reset(self):
        for s in (self.st, self.sv, self.sa, self.sy, self.sg, self.scc,
                  self.sgx, *self.sgp):
            s.v = 0
        self.mark = {}

    def op(self, ek, fname, *args, inc=None, incv=1, **kw):
        if ek in ("v", "a"):
            s = self.sv if ek == "v" else self.sa
            assert inc is None or inc is s
            pre = s.v
            s.v += 1
            if self.cur == ek:
                self.h.wait_ge(s.h, pre)
                getattr(self.h, fname)(*args, **kw).then_inc(s.h, 1)
            return
        if inc is not None:
            inc.v += incv
        if self.cur == ek:
            inst = getattr(self.h, fname)(*args, **kw)
            if inc is not None:
                inst.then_inc(inc.h, incv)

    def wait(self, ek, sem, v=None):
        if self.cur == ek:
            self.h.wait_ge(sem.h, sem.v if v is None else v)

    def gslot(self, slot, out, in_, key):
        self.op("g", "dma_start", out=out, in_=in_, inc=self.sgp[slot], incv=16)
        self.mark[key] = (slot, self.sgp[slot].v)

    def gwait(self, ek, key):
        slot, v = self.mark[key]
        self.wait(ek, self.sgp[slot], v)

    def barrier(self, engines="tva"):
        for ek in engines:
            for s in (self.st, self.sv, self.sa, self.sy, self.sg,
                      self.sgx, *self.sgp):
                if s.v > 0:
                    self.wait(ek, s)



def _consts_phase(C):
    cw = C.C_cw
    for name, dst in (
        ("identf", C.C_identf), ("difft", C.C_difft), ("maskt", C.C_maskt),
        ("ldr", C.C_ldr), ("sel16", C.C_sel16), ("ldc", C.C_ldc),
        ("onesrow", C.C_onesrow), ("onescol", C.C_onescol),
        ("cposrow", C.C_cposrow), ("revtrow", C.C_revtrow),
        ("cdlrow", C.C_cdlrow), ("scmm", C.C_scmm), ("scmk", C.C_scmk),
        ("nw", C.C_nw),
    ):
        sl = tuple(slice(0, s) for s in dst.shape)
        C.op("y", "dma_start", out=dst[sl], in_=C.din[name][sl], inc=C.sy, incv=16)
    C.op("g", "dma_start", out=C.C_identb[:, :], in_=C.din["identf"][:, :], inc=C.sg, incv=16)
    C.op("g", "dma_start", out=C.C_wg2[:, :], in_=C.din["wg2"][:, :], inc=C.sg, incv=16)
    C.op("g", "dma_start", out=C.C_gwt[:, :],
         in_=C.din["gwt"].ap().rearrange("p a b -> p (a b)"), inc=C.sg, incv=16)

    C.wait("t", C.sy)
    C.op("t", "matmul", C.PA[:, 0:16], C.C_onesrow[0:1, :], C.C_ldr[0:1, :],
         start=True, stop=True)
    C.op("t", "matmul", C.PA[:, 16:32], C.C_revtrow[0:1, :], C.C_ldr[0:1, :],
         start=True, stop=True)
    C.op("t", "matmul", C.PA[0:1, 48:176], C.C_ldc[0:H, 0:1], C.C_sel16[0:H, :],
         start=True, stop=True, inc=C.st)
    C.wait("a", C.st)
    C.op("a", "activation", cw[:, 0:16], C.PA[:, 0:16], AF.Copy)
    C.op("a", "activation", cw[:, 16:32], C.PA[:, 16:32], AF.Exp)
    C.op("a", "activation", cw[0:1, 192:320], C.PA[0:1, 48:176], AF.Copy, inc=C.sa)
    C.wait("v", C.sa)
    C.op("v", "tensor_scalar", cw[0:1, 320:336], C.C_ldr[0:1, :], 128.0, None, ALU.mult)
    C.op("v", "tensor_mul", cw[0:1, 336:464], cw[0:1, 192:320], C.C_cdlrow[0:1, :], inc=C.sv)
    C.wait("a", C.sv)
    C.op("a", "activation", cw[0:1, 464:480], cw[0:1, 320:336], AF.Exp)
    C.op("a", "activation", cw[0:1, 480:608], cw[0:1, 336:464], AF.Exp, inc=C.sa)
    C.wait("t", C.sa)
    C.op("t", "matmul", C.PA[:, 48:176], C.C_onesrow[0:1, :], cw[0:1, 480:608],
         start=True, stop=True)
    C.op("t", "matmul", C.PA[:, 176:192], C.C_onesrow[0:1, :], cw[0:1, 464:480],
         start=True, stop=True, inc=C.st)
    C.wait("v", C.st)
    C.op("v", "tensor_copy", cw[:, 64:192], C.PA[:, 48:176])
    C.op("v", "tensor_copy", cw[:, 48:64], C.PA[:, 176:192], inc=C.sv)


def _p0_layernorm(C):
    for dc in range(DC):
        C.op("y", "dma_start", out=C.B_acc[:, 1024 * dc:1024 * (dc + 1)],
             in_=C.din["x"][dc], inc=C.sy, incv=16)
        C.mark[("xld", dc)] = C.sy.v
    for dc in range(DC):
        C.wait("a", C.sy, C.mark[("xld", dc)])
        if dc >= 2:
            C.wait("a", C.st, C.mark[("p0mm", dc - 2)])
        C.op("a", "activation", C.Xsq[dc % 2][:, :], C.B_acc[:, 1024 * dc:1024 * (dc + 1)],
             AF.Square, inc=C.sa)
        C.mark[("p0sq", dc)] = C.sa.v
        C.wait("t", C.sa, C.mark[("p0sq", dc)])
        C.wait("t", C.sy, C.mark[("xld", dc)])
        for tt in range(2):
            C.op("t", "matmul", C.PB[0:1, 512 * tt:512 * (tt + 1)],
                 C.C_onescol[:, 0:1],
                 C.B_acc[:, 1024 * dc + 512 * tt:1024 * dc + 512 * (tt + 1)],
                 start=(dc == 0), stop=(dc == DC - 1))
            C.op("t", "matmul", C.PA[0:1, 512 * tt:512 * (tt + 1)],
                 C.C_onescol[:, 0:1], C.Xsq[dc % 2][:, 512 * tt:512 * (tt + 1)],
                 start=(dc == 0), stop=(dc == DC - 1),
                 inc=(C.st if tt == 1 else None))
        C.mark[("p0mm", dc)] = C.st.v
    C.wait("v", C.st)
    C.op("v", "tensor_scalar", C.Emean[0:1, :], C.PB[0:1, 0:1024], 1.0 / D, None, ALU.mult)
    C.op("v", "tensor_scalar", C.Evar[0:1, :], C.PA[0:1, 0:1024], 1.0 / D, None, ALU.mult)
    C.op("v", "tensor_mul", C.Emrs[0:1, :], C.Emean[0:1, :], C.Emean[0:1, :])
    C.op("v", "tensor_sub", C.Evar[0:1, :], C.Evar[0:1, :], C.Emrs[0:1, :], inc=C.sv)
    C.op("v", "tensor_scalar", C.Evar[0:1, :], C.Evar[0:1, :], EPS, None, ALU.add, inc=C.sv)
    C.wait("a", C.sv)
    C.op("a", "activation", C.Evar[0:1, :], C.Evar[0:1, :], AF.Sqrt, inc=C.sa)
    C.wait("v", C.sa)
    C.op("v", "reciprocal", C.Evar[0:1, :], C.Evar[0:1, :])
    C.op("v", "tensor_mul", C.Emrs[0:1, :], C.Emean[0:1, :], C.Evar[0:1, :], inc=C.sv)
    C.wait("t", C.sv)
    for tt in range(2):
        C.op("t", "matmul", C.PA[:, 512 * tt:512 * (tt + 1)], C.C_onesrow[0:1, :],
             C.Evar[0:1, 512 * tt:512 * (tt + 1)], start=True, stop=True)
        C.op("t", "matmul", C.PA[:, 1024 + 512 * tt:1024 + 512 * (tt + 1)],
             C.C_onesrow[0:1, :], C.Emrs[0:1, 512 * tt:512 * (tt + 1)],
             start=True, stop=True, inc=(C.st if tt == 1 else None))
    C.wait("v", C.st)
    C.op("v", "tensor_copy", C.Brstd[:, :], C.PA[:, 0:1024])
    C.op("v", "tensor_copy", C.Bmrs[:, :], C.PA[:, 1024:2048], inc=C.sv)
    for dc in range(DC):
        C.op("v", "tensor_mul", C.Xsq[1][:, :], C.B_acc[:, 1024 * dc:1024 * (dc + 1)],
             C.Brstd[:, :])
        C.op("v", "tensor_sub", C.B_xn[:, 1024 * dc:1024 * (dc + 1)], C.Xsq[1][:, :],
             C.Bmrs[:, :], inc=(C.sv if dc == DC - 1 else None))
    C.mark["xn"] = C.sv.v


def _projection(C, wname, dst, pre_evac_wait=None, post=None):
    C.barrier("g")
    C.barrier("t")
    NS = len(C.WSLAB)
    for ec in range(DC):
        if ec >= NS:
            C.wait("g", C.st, C.mark[("proj_mm", wname, ec - NS)])
        C.gslot(ec % NS, C.WSLAB[ec % NS][:, :], C.din[wname][ec],
                ("proj_ld", wname, ec))
        C.gwait("t", ("proj_ld", wname, ec))
        if ec >= 2:
            C.wait("t", C.sa, C.mark[("proj_ev", wname, ec - 2)])
        for dc in range(DC):
            for tt in range(2):
                bank = 1024 * (ec % 2) + 512 * tt
                C.op("t", "matmul", C.PA[:, bank:bank + 512],
                     C.WSLAB[ec % NS][:, 128 * dc:128 * (dc + 1)],
                     C.B_xn[:, 1024 * dc + 512 * tt:1024 * dc + 512 * (tt + 1)],
                     start=(dc == 0), stop=(dc == DC - 1),
                     inc=(C.st if (tt == 1 and dc == DC - 1) else None))
        C.mark[("proj_mm", wname, ec)] = C.st.v

        C.wait("a", C.st, C.mark[("proj_mm", wname, ec)])
        if pre_evac_wait is not None:
            pre_evac_wait(ec)
        half = 1024 * (ec % 2)
        C.op("a", "activation", dst(ec)[:, 0:512], C.PA[:, half:half + 512], AF.Silu)
        C.op("a", "activation", dst(ec)[:, 512:1024], C.PA[:, half + 512:half + 1024],
             AF.Silu, inc=C.sa)
        C.mark[("proj_ev", wname, ec)] = C.sa.v
        if post is not None:
            post(ec)


def _p1_kv(C):
    _projection(C, "wk", lambda ec: C.B_k[:, 1024 * ec:1024 * (ec + 1)])

    def v_pre_wait(ec):
        if ec >= 2:
            C.wait("a", C.st, C.mark[("vtr", ec - 2)])

    def v_post(ec):
        C.wait("t", C.sa, C.mark[("proj_ev", "wv", ec)])
        if ec >= 2:
            C.wait("t", C.sv, C.mark[("vtr_ev", ec - 2)])
        half = 1024 * (ec % 2)
        for ch in range(NCH):
            C.op("t", "transpose", C.PC[:, half + 128 * ch:half + 128 * (ch + 1)],
                 C.VST[ec % 2][:, 128 * ch:128 * (ch + 1)], C.C_identb[:, :],
                 inc=(C.st if ch == NCH - 1 else None))
        C.mark[("vtr", ec)] = C.st.v
        C.wait("v", C.st, C.mark[("vtr", ec)])
        for ch in range(NCH):
            C.op("v", "tensor_copy",
                 C.B_v[:, 2048 * ch + 128 * ec:2048 * ch + 128 * (ec + 1)],
                 C.PC[:, half + 128 * ch:half + 128 * (ch + 1)],
                 inc=(C.sv if ch == NCH - 1 else None))
        C.mark[("vtr_ev", ec)] = C.sv.v

    _projection(C, "wv", lambda ec: C.VST[ec % 2], pre_evac_wait=v_pre_wait, post=v_post)
    C.barrier("tva")


def _a2_m_collective(C):
    for ch in range(NCH):
        if ch >= 1:
            C.wait("t", C.sv, C.mark[("a2ev", ch - 1)])
        for h in range(H):
            C.op("t", "transpose", C.PC[:, 128 * h:128 * (h + 1)],
                 C.B_k[:, 1024 * h + 128 * ch:1024 * h + 128 * (ch + 1)],
                 C.C_identb[:, :], inc=(C.st if h == H - 1 else None))
        C.mark[("a2tr", ch)] = C.st.v
        C.wait("v", C.st, C.mark[("a2tr", ch)])
        for h in range(H):
            C.op("v", "tensor_scalar", C.KTMall[ch][:, 128 * h:128 * (h + 1)],
                 C.PC[:, 128 * h:128 * (h + 1)], C.C_cw[:, 16 + h:17 + h], None, ALU.mult)
            C.op("v", "tensor_scalar", C.VSall[ch][:, 128 * h:128 * (h + 1)],
                 C.B_v[:, 2048 * ch + 128 * h:2048 * ch + 128 * (h + 1)],
                 C.C_cw[:, 64 + 8 * h + ch:64 + 8 * h + ch + 1], None, ALU.mult,
                 inc=(C.sv if h == H - 1 else None))
        C.mark[("a2ev", ch)] = C.sv.v
    C.wait("t", C.sv)
    for h in range(H):
        for ch in range(NCH):
            C.op("t", "matmul", C.PA[:, 128 * h:128 * (h + 1)],
                 C.KTMall[ch][:, 128 * h:128 * (h + 1)],
                 C.VSall[ch][:, 128 * h:128 * (h + 1)],
                 start=(ch == 0), stop=(ch == NCH - 1),
                 inc=(C.st if (h == H - 1 and ch == NCH - 1) else None))
    C.wait("v", C.st)
    C.op("v", "tensor_copy", C.Mst[:, :], C.PA[:, :], inc=C.sv)
    C.mark["m_done"] = C.sv.v
    C.wait("y", C.sv, C.mark["m_done"])
    C.op("y", "dma_start", out=C.dcin.ap().rearrange("(h p) e -> p h e", p=128),
         in_=C.Mst[:, :].rearrange("p (h e) -> p h e", h=H), inc=C.sy, incv=16)
    C.mark["cin"] = C.sy.v
    C.wait("g", C.sy, C.mark["cin"])
    C.op("g", "collective_compute", "AllGather", ALU.bypass,
         replica_groups=[[0, 1, 2, 3], [4, 5, 6, 7]],
         ins=[C.dcin.ap().opt()], outs=[C.dcout.ap().opt()], inc=C.scc)
    C.mark["cc"] = C.scc.v


def _p3_q_g1(C):
    _projection(C, "wq", lambda ec: C.B_q[:, 1024 * ec:1024 * (ec + 1)])
    C.wait("g", C.st, C.mark[("proj_mm", "wq", DC - 1)])
    C.op("g", "dma_start", out=C.G1W[:, :],
         in_=C.din["wg1"].ap().rearrange("p a b -> p (a b)"), inc=C.sgx, incv=16)
    C.barrier("t")
    C.wait("t", C.sgx)
    for dc in range(DC):
        for tt in range(2):
            C.op("t", "matmul", C.PB[:, 512 * tt:512 * (tt + 1)],
                 C.G1W[:, 128 * dc:128 * (dc + 1)],
                 C.B_xn[:, 1024 * dc + 512 * tt:1024 * dc + 512 * (tt + 1)],
                 start=(dc == 0), stop=(dc == DC - 1),
                 inc=(C.st if (tt == 1 and dc == DC - 1) else None))
    C.wait("v", C.st)
    C.op("v", "tensor_copy", C.G1[:, :], C.PB[:, 0:1024], inc=C.sv)
    C.mark["g1"] = C.sv.v


def _p3b_consts2_sstart(C):
    cw = C.C_cw
    C.barrier("tvay")
    for h in range(H):
        C.op("v", "tensor_scalar", cw[0:1, 336:464], C.C_cposrow[0:1, :],
             C.C_ldr[0:1, h:h + 1], None, ALU.mult, inc=C.sv)
        C.mark[("qr1", h)] = C.sv.v
        C.wait("a", C.sv, C.mark[("qr1", h)])
        C.op("a", "activation", cw[0:1, 464:592], cw[0:1, 336:464], AF.Exp, inc=C.sa)
        C.mark[("qr2", h)] = C.sa.v
        C.wait("t", C.sa, C.mark[("qr2", h)])
        if h >= 4:
            C.wait("t", C.sv, C.mark[("qdbev", h - 4)])
        C.op("t", "matmul", C.PA[:, 128 * (h % 4):128 * (h % 4 + 1)],
             C.C_onesrow[0:1, :], cw[0:1, 464:592], start=True, stop=True, inc=C.st)
        C.mark[("qdbmm", h)] = C.st.v
        C.wait("v", C.st, C.mark[("qdbmm", h)])
        C.op("v", "tensor_copy", C.QDB[:, 128 * h:128 * (h + 1)],
             C.PA[:, 128 * (h % 4):128 * (h % 4 + 1)], inc=C.sv)
        C.mark[("qdbev", h)] = C.sv.v
    for h in range(H):
        C.op("v", "tensor_scalar", C.OG[:, 0:128], C.C_difft[:, :], cw[:, h:h + 1],
             None, ALU.mult, inc=C.sv)
        C.mark[("dtm1", h)] = C.sv.v
        C.wait("a", C.sv, C.mark[("dtm1", h)])
        C.op("a", "activation", C.OG[:, 128:256], C.OG[:, 0:128], AF.Exp, inc=C.sa)
        C.mark[("dtm2", h)] = C.sa.v
        C.wait("v", C.sa, C.mark[("dtm2", h)])
        C.op("v", "tensor_mul", C.DTm[:, 128 * h:128 * (h + 1)], C.OG[:, 128:256],
             C.C_maskt[:, :], inc=C.sv)
    for jp in range(3):
        C.op("v", "tensor_scalar", cw[:, 832:848], cw[:, 0:16], C.C_scmm[:, jp:jp + 1],
             None, ALU.mult, inc=C.sv)
        C.mark[("cf1", jp)] = C.sv.v
        C.wait("a", C.sv, C.mark[("cf1", jp)])
        C.op("a", "activation", cw[:, 848:864], cw[:, 832:848], AF.Exp, inc=C.sa)
        C.mark[("cf2", jp)] = C.sa.v
        C.wait("v", C.sa, C.mark[("cf2", jp)])
        C.op("v", "tensor_scalar", cw[:, 736 + 16 * jp:752 + 16 * jp], cw[:, 848:864],
             C.C_scmk[:, jp:jp + 1], None, ALU.mult, inc=C.sv)
    C.mark["coef"] = C.sv.v
    C.wait("y", C.scc, C.mark["cc"])
    for jp in range(3):
        if jp >= 1:
            C.wait("y", C.sv, C.mark[("sacc", jp - 1)])
        C.op("y", "dma_start",
             out=C.Mst[:, :].rearrange("p (h e) -> p h e", h=H),
             in_=C.dcout.ap()[2048 * jp:2048 * (jp + 1), :].rearrange(
                 "(h p) e -> p h e", p=128),
             inc=C.sy, incv=16)
        C.mark[("smld", jp)] = C.sy.v
        C.wait("v", C.sy, C.mark[("smld", jp)])
        for h in range(H):
            if jp == 0:
                C.op("v", "tensor_scalar", C.S[:, 128 * h:128 * (h + 1)],
                     C.Mst[:, 128 * h:128 * (h + 1)], cw[:, 736 + h:737 + h],
                     None, ALU.mult, inc=(C.sv if h == H - 1 else None))
            else:
                C.op("v", "tensor_scalar", C.OG[:, 0:128],
                     C.Mst[:, 128 * h:128 * (h + 1)],
                     cw[:, 736 + 16 * jp + h:737 + 16 * jp + h], None, ALU.mult)
                C.op("v", "tensor_add", C.S[:, 128 * h:128 * (h + 1)],
                     C.S[:, 128 * h:128 * (h + 1)], C.OG[:, 0:128],
                     inc=(C.sv if h == H - 1 else None))
        C.mark[("sacc", jp)] = C.sv.v
    C.op("v", "tensor_copy", C.S_bf[:, :], C.S[:, :], inc=C.sv)
    C.mark["sbf"] = C.sv.v
    C.op("v", "tensor_scalar", cw[0:1, 320:336], C.C_ldr[0:1, :], 128.0, None,
         ALU.mult, inc=C.sv)
    C.mark["cdr0"] = C.sv.v
    C.wait("a", C.sv, C.mark["cdr0"])
    C.op("a", "activation", cw[0:1, 592:608], cw[0:1, 320:336], AF.Exp, inc=C.sa)
    C.mark["cdr1"] = C.sa.v
    C.wait("v", C.sa, C.mark["cdr1"])
    for h in range(H):
        C.op("v", "tensor_scalar", cw[0:1, 336:464], C.C_onesrow[0:1, :],
             cw[0:1, 592 + h:593 + h], None, ALU.mult, inc=C.sv)
        C.mark[("cdr", h)] = C.sv.v
        C.wait("t", C.sv, C.mark[("cdr", h)])
        if h >= 4:
            C.wait("t", C.sv, C.mark[("cdbe", h - 4)])
        C.op("t", "matmul", C.PA[:, 128 * (h % 4):128 * (h % 4 + 1)],
             C.C_onesrow[0:1, :], cw[0:1, 336:464], start=True, stop=True, inc=C.st)
        C.mark[("cdbm", h)] = C.st.v
        C.wait("v", C.st, C.mark[("cdbm", h)])
        C.op("v", "tensor_copy", C.CDB[:, 128 * h:128 * (h + 1)],
             C.PA[:, 128 * (h % 4):128 * (h % 4 + 1)], inc=C.sv)
        C.mark[("cdbe", h)] = C.sv.v


def _a3_attention(C):
    cw = C.C_cw
    C.barrier("t")
    C.mark["pa_free"] = 0
    C.mark["pb_free"] = 0
    C.mark["pc_free"] = 0
    for ch in range(NCH):
        C.wait("t", C.sv, C.mark["pa_free"])
        for et in range(4):
            C.op("t", "matmul", C.PA[:, 512 * et:512 * (et + 1)],
                 C.G1[:, 128 * ch:128 * (ch + 1)],
                 C.C_wg2[:, 512 * et:512 * (et + 1)], start=True, stop=True,
                 inc=(C.st if et == 3 else None))
        C.mark["gate_mm"] = C.st.v
        C.wait("a", C.st, C.mark["gate_mm"])
        C.op("a", "activation", C.SIG[:, :], C.PA[:, :], AF.Sigmoid, inc=C.sa)
        C.mark["sig"] = C.sa.v
        C.wait("t", C.sv, C.mark["pc_free"])
        for h in range(H):
            C.op("t", "transpose", C.PC[:, 128 * h:128 * (h + 1)],
                 C.B_k[:, 1024 * h + 128 * ch:1024 * h + 128 * (ch + 1)],
                 C.C_identb[:, :], inc=(C.st if h == H - 1 else None))
        C.mark["ktr"] = C.st.v
        C.wait("v", C.st, C.mark["ktr"])
        for h in range(H):
            C.op("v", "tensor_scalar", C.KTM[:, 128 * h:128 * (h + 1)],
                 C.PC[:, 128 * h:128 * (h + 1)], cw[:, 16 + h:17 + h], None, ALU.mult,
                 inc=(C.sv if h == H - 1 else None))
        C.mark["ktm"] = C.sv.v
        C.wait("t", C.sv, C.mark["pb_free"])
        for w in range(2):
            if w == 1:
                C.wait("t", C.sv, C.mark[("scev", 0)])
            for i in range(8):
                h = 8 * w + i
                C.op("t", "matmul", C.PB[:, 128 * i:128 * (i + 1)],
                     C.B_k[:, 1024 * h + 128 * ch:1024 * h + 128 * (ch + 1)],
                     C.B_q[:, 1024 * h + 128 * ch:1024 * h + 128 * (ch + 1)],
                     start=True, stop=True, inc=(C.st if i == 7 else None))
            C.mark[("sc", w)] = C.st.v
            C.wait("v", C.st, C.mark[("sc", w)])
            C.op("v", "tensor_mul", C.STm[:, 1024 * w:1024 * (w + 1)],
                 C.PB[:, 0:1024], C.DTm[:, 1024 * w:1024 * (w + 1)], inc=C.sv)
            C.mark[("scev", w)] = C.sv.v
        C.op("v", "tensor_mul",
             C.QS[:, :].rearrange("p (h e) -> p h e", h=H),
             C.B_q[:, :].rearrange("p (h t) -> p h t", h=H)[:, :, 128 * ch:128 * (ch + 1)],
             C.QDB[:, :].rearrange("p (h e) -> p h e", h=H), inc=C.sv)
        C.mark["qs"] = C.sv.v
        C.wait("t", C.sv, C.mark["qs"])
        C.wait("t", C.sa, C.mark["sig"])
        for h in range(H):
            C.op("t", "matmul", C.PA[:, 128 * h:128 * (h + 1)],
                 C.STm[:, 128 * h:128 * (h + 1)],
                 C.B_v[:, 2048 * ch + 128 * h:2048 * ch + 128 * (h + 1)],
                 start=True, stop=False)
            C.op("t", "matmul", C.PA[:, 128 * h:128 * (h + 1)],
                 C.QS[:, 128 * h:128 * (h + 1)],
                 C.S_bf[:, 128 * h:128 * (h + 1)],
                 start=False, stop=True, inc=(C.st if h == H - 1 else None))
        C.mark["omm"] = C.st.v
        C.wait("v", C.st, C.mark["omm"])
        C.op("v", "tensor_mul", C.OG[:, :], C.PA[:, :], C.SIG[:, :], inc=C.sv)
        C.mark["og"] = C.sv.v
        if ch < NCH - 1:
            for w in range(2):
                C.wait("t", C.sv, C.mark[("scev", 1)] if w == 0 else C.mark[("sup", 0)])
                for i in range(8):
                    h = 8 * w + i
                    C.op("t", "matmul", C.PB[:, 128 * i:128 * (i + 1)],
                         C.KTM[:, 128 * h:128 * (h + 1)],
                         C.B_v[:, 2048 * ch + 128 * h:2048 * ch + 128 * (h + 1)],
                         start=True, stop=True, inc=(C.st if i == 7 else None))
                C.mark[("kv", w)] = C.st.v
                C.wait("v", C.st, C.mark[("kv", w)])
                C.op("v", "tensor_mul", C.S[:, 1024 * w:1024 * (w + 1)],
                     C.S[:, 1024 * w:1024 * (w + 1)], C.CDB[:, 1024 * w:1024 * (w + 1)])
                C.op("v", "tensor_add", C.S[:, 1024 * w:1024 * (w + 1)],
                     C.S[:, 1024 * w:1024 * (w + 1)], C.PB[:, 0:1024], inc=C.sv)
                C.mark[("sup", w)] = C.sv.v
            C.mark["pb_free"] = C.mark[("sup", 1)]
            C.op("v", "tensor_copy", C.S_bf[:, :], C.S[:, :], inc=C.sv)
            C.mark["sbf"] = C.sv.v
        else:
            C.mark["pb_free"] = C.mark[("scev", 1)]
        C.wait("a", C.sv, C.mark["og"])
        for h in range(H):
            C.op("a", "activation", C.OTM[:, 128 * h:128 * (h + 1)],
                 C.OG[:, 128 * h:128 * (h + 1)], AF.Square,
                 accum_out=cw[:, 864 + h:865 + h],
                 inc=(C.sa if h == H - 1 else None))
        C.mark["sq"] = C.sa.v
        C.mark["pa_free"] = C.mark["og"]
        C.wait("v", C.sa, C.mark["sq"])
        C.op("v", "tensor_scalar", cw[:, 880:896], cw[:, 864:880], 1.0 / 128.0, EPS,
             ALU.mult, ALU.add, inc=C.sv)
        C.mark["rsq"] = C.sv.v
        C.wait("a", C.sv, C.mark["rsq"])
        C.op("a", "activation", cw[:, 880:896], cw[:, 880:896], AF.Sqrt,
             inc=C.sa)
        C.mark["rsqrt"] = C.sa.v
        C.wait("v", C.sa, C.mark["rsqrt"])
        C.op("v", "reciprocal", cw[:, 880:896], cw[:, 880:896])
        for h in range(H):
            C.op("v", "tensor_scalar", C.OTM[:, 128 * h:128 * (h + 1)],
                 C.OG[:, 128 * h:128 * (h + 1)], cw[:, 880 + h:881 + h], None, ALU.mult,
                 inc=(C.sv if h == H - 1 else None))
        C.mark["otm"] = C.sv.v
        C.wait("t", C.sv, C.mark["otm"])
        C.wait("t", C.sv, C.mark["ktm"])
        for h in range(H):
            C.op("t", "transpose", C.PC[:, 128 * h:128 * (h + 1)],
                 C.OTM[:, 128 * h:128 * (h + 1)], C.C_identb[:, :],
                 inc=(C.st if h == H - 1 else None))
        C.mark["otr"] = C.st.v
        C.wait("v", C.st, C.mark["otr"])
        for h in range(H):
            C.op("v", "tensor_scalar",
                 C.B_q[:, 1024 * h + 128 * ch:1024 * h + 128 * (ch + 1)],
                 C.PC[:, 128 * h:128 * (h + 1)], C.C_nw[:, h:h + 1], None, ALU.mult,
                 inc=(C.sv if h == H - 1 else None))
        C.mark["pc_free"] = C.sv.v


def _a4_wo(C):
    C.barrier("tva")
    C.barrier("g")
    NS = len(C.WOSLAB)
    for ec in range(DC):
        if ec >= NS:
            C.wait("g", C.st, C.mark[("wo_mm", ec - NS)])
        C.gslot(ec % NS, C.WOSLAB[ec % NS][:, :], C.din["wo"][ec], ("wo_ld", ec))
        C.gwait("t", ("wo_ld", ec))
        if ec >= 2:
            C.wait("t", C.sv, C.mark[("wo_ev", ec - 2)])
        for dc in range(DC):
            for tt in range(2):
                bank = 1024 * (ec % 2) + 512 * tt
                C.op("t", "matmul", C.PA[:, bank:bank + 512],
                     C.WOSLAB[ec % NS][:, 128 * dc:128 * (dc + 1)],
                     C.B_q[:, 1024 * dc + 512 * tt:1024 * dc + 512 * (tt + 1)],
                     start=(dc == 0), stop=(dc == DC - 1),
                     inc=(C.st if (tt == 1 and dc == DC - 1) else None))
        C.mark[("wo_mm", ec)] = C.st.v
        C.wait("v", C.st, C.mark[("wo_mm", ec)])
        half = 1024 * (ec % 2)
        C.op("v", "tensor_copy", C.B_acc[:, 1024 * ec:1024 * ec + 512],
             C.PA[:, half:half + 512])
        C.op("v", "tensor_copy", C.B_acc[:, 1024 * ec + 512:1024 * (ec + 1)],
             C.PA[:, half + 512:half + 1024], inc=C.sv)
        C.mark[("wo_ev", ec)] = C.sv.v


def _p4_router(C):
    C.barrier("tv")
    r = C.Rsc
    for tc in range(NCH):
        if tc >= 1:
            C.wait("t", C.sv, C.mark[("rt1", tc - 1)])
            C.wait("t", C.sa, C.mark[("rt2", tc - 1)])
        for dc in range(DC):
            C.op("t", "matmul", C.PA[:, 0:8],
                 C.B_xn[:, 1024 * dc + 128 * tc:1024 * dc + 128 * (tc + 1)],
                 C.C_gwt[:, 8 * dc:8 * (dc + 1)],
                 start=(dc == 0), stop=(dc == DC - 1),
                 inc=(C.st if dc == DC - 1 else None))
        C.mark[("lg_mm", tc)] = C.st.v
        C.wait("v", C.st, C.mark[("lg_mm", tc)])
        C.op("v", "reduce_max", r[:, 32:33], C.PA[:, 0:8], axis=AX)
        C.op("v", "tensor_scalar", r[:, 33:34], r[:, 32:33], -1.0, None, ALU.mult,
             inc=C.sv)
        C.mark[("rt1", tc)] = C.sv.v
        C.wait("a", C.sv, C.mark[("rt1", tc)])
        C.op("a", "activation", r[:, 0:8], C.PA[:, 0:8], AF.Exp, bias=r[:, 33:34],
             inc=C.sa)
        C.mark[("rt2", tc)] = C.sa.v
        C.wait("v", C.sa, C.mark[("rt2", tc)])
        C.op("v", "reduce_sum", r[:, 32:33], r[:, 0:8], axis=AX)
        C.op("v", "reciprocal", r[:, 33:34], r[:, 32:33])
        C.op("v", "tensor_scalar", r[:, 0:8], r[:, 0:8], r[:, 33:34], None, ALU.mult)
        C.op("v", "tensor_tensor", r[:, 24:28], r[:, 0:8:2], r[:, 1:8:2], ALU.max)
        C.op("v", "reduce_max", r[:, 32:33], r[:, 24:28], axis=AX)
        C.op("v", "tensor_scalar", r[:, 28:32], r[:, 24:28], r[:, 32:33], None, ALU.is_ge)
        C.op("v", "tensor_scalar", r[:, 28:32], r[:, 28:32], BIG, None, ALU.mult)
        C.op("v", "tensor_sub", r[:, 28:32], r[:, 24:28], r[:, 28:32])
        C.op("v", "reduce_max", r[:, 33:34], r[:, 28:32], axis=AX)
        C.op("v", "tensor_scalar", r[:, 24:28], r[:, 24:28], r[:, 33:34], None, ALU.is_ge)
        C.op("v", "tensor_copy", r[:, 8:16:2], r[:, 24:28])
        C.op("v", "tensor_copy", r[:, 9:16:2], r[:, 24:28])
        C.op("v", "tensor_mul", r[:, 8:16], r[:, 0:8], r[:, 8:16])
        C.op("v", "reduce_max", r[:, 32:33], r[:, 8:16], axis=AX)
        C.op("v", "tensor_scalar", r[:, 16:24], r[:, 8:16], r[:, 32:33], None, ALU.is_ge)
        C.op("v", "tensor_scalar", r[:, 16:24], r[:, 16:24], BIG, None, ALU.mult)
        C.op("v", "tensor_sub", r[:, 16:24], r[:, 8:16], r[:, 16:24])
        C.op("v", "reduce_max", r[:, 33:34], r[:, 16:24], axis=AX)
        C.op("v", "tensor_scalar", r[:, 16:24], r[:, 8:16], r[:, 33:34], None, ALU.is_ge)
        C.op("v", "tensor_mul", C.Rwm[:, 8 * tc:8 * (tc + 1)], r[:, 8:16], r[:, 16:24],
             inc=C.sv)
        C.mark[("wm", tc)] = C.sv.v
    for tc in range(NCH):
        C.wait("t", C.sv, C.mark[("wm", tc)])
        if tc >= 1:
            C.wait("t", C.sv, C.mark[("wms", tc - 1)])
        C.op("t", "transpose", C.PB[0:8, 0:128], C.Rwm[:, 8 * tc:8 * (tc + 1)],
             C.C_identf[:, :], inc=C.st)
        C.mark[("wmt", tc)] = C.st.v
        C.wait("v", C.st, C.mark[("wmt", tc)])
        C.op("v", "tensor_copy", C.WMS[0:8, 128 * tc:128 * (tc + 1)], C.PB[0:8, 0:128],
             inc=C.sv)
        C.mark[("wms", tc)] = C.sv.v
    C.wait("y", C.sv, C.mark[("wms", NCH - 1)])
    C.op("y", "dma_start", out=C.dwms[0:8, :], in_=C.WMS[0:8, :], inc=C.sy, incv=16)
    C.mark["wmsd"] = C.sy.v


def _expert_ffn(C, n):
    C.wait("y", C.sy, C.mark["wmsd"])
    if ("wb_mm",) in C.mark:
        C.wait("y", C.st, C.mark[("wb_mm",)])
    C.op("y", "dma_start", out=C.WMR[0:1, :], in_=C.dwms[n, :][None, :], inc=C.sy, incv=16)
    C.mark["wmr"] = C.sy.v
    C.wait("t", C.sy, C.mark["wmr"])
    C.wait("t", C.sv)
    for tt in range(2):
        C.op("t", "matmul", C.PB[:, 512 * tt:512 * (tt + 1)], C.C_onesrow[0:1, :],
             C.WMR[0:1, 512 * tt:512 * (tt + 1)], start=True, stop=True,
             inc=(C.st if tt == 1 else None))
    C.mark[("wb_mm",)] = C.st.v
    C.wait("v", C.st, C.mark[("wb_mm",)])
    C.op("v", "tensor_copy", C.WB[:, :], C.PB[:, 0:1024], inc=C.sv)
    C.mark["wb"] = C.sv.v

    NSG = len(C.GUS)
    NSD = len(C.DS)
    for fc in range(FC):
        if fc >= NSG:
            C.wait("g", C.st, C.mark[("gu_mm", fc - NSG)])
        elif ("gu_mm", FC - 1) in C.mark:
            C.wait("g", C.st, C.mark[("gu_mm", FC - 1)])
        C.gslot(fc % NSG, C.GUS[fc % NSG][:, :], C.din["guw"][n, fc], ("gu_ld", fc))
        C.gwait("t", ("gu_ld", fc))
        if fc >= 1:
            C.wait("t", C.sv, C.mark[("uss_ev", fc - 1)])
            C.wait("t", C.sa, C.mark[("sg_ev", fc - 1)])
        elif ("uss_ev", FC - 1) in C.mark:
            C.wait("t", C.sv, C.mark[("uss_ev", FC - 1)])
            C.wait("t", C.sa, C.mark[("sg_ev", FC - 1)])
        for dc in range(DC):
            for tt in range(2):
                C.op("t", "matmul", C.PA[:, 512 * tt:512 * (tt + 1)],
                     C.GUS[fc % NSG][:, 128 * dc:128 * (dc + 1)],
                     C.B_xn[:, 1024 * dc + 512 * tt:1024 * dc + 512 * (tt + 1)],
                     start=(dc == 0), stop=(dc == DC - 1))
        for dc in range(DC):
            for tt in range(2):
                C.op("t", "matmul", C.PA[:, 1024 + 512 * tt:1024 + 512 * (tt + 1)],
                     C.GUS[fc % NSG][:, 2048 + 128 * dc:2048 + 128 * (dc + 1)],
                     C.B_xn[:, 1024 * dc + 512 * tt:1024 * dc + 512 * (tt + 1)],
                     start=(dc == 0), stop=(dc == DC - 1),
                     inc=(C.st if (tt == 1 and dc == DC - 1) else None))
        C.mark[("gu_mm", fc)] = C.st.v

        C.wait("a", C.st, C.mark[("gu_mm", fc)])
        if ("a_ev", fc - 2) in C.mark:
            C.wait("a", C.sv, C.mark[("a_ev", fc - 2)])
        C.op("a", "activation", C.SGS[fc % 2][:, :], C.PA[:, 0:1024], AF.Silu, inc=C.sa)
        C.mark[("sg_ev", fc)] = C.sa.v
        C.wait("v", C.st, C.mark[("gu_mm", fc)])
        C.wait("v", C.sv, C.mark["wb"])
        C.op("v", "tensor_mul", C.USS[fc % 2][:, :], C.PA[:, 1024:2048], C.WB[:, :],
             inc=C.sv)
        C.mark[("uss_ev", fc)] = C.sv.v
        C.wait("v", C.sa, C.mark[("sg_ev", fc)])
        C.op("v", "tensor_mul", C.A_E[fc][:, :], C.SGS[fc % 2][:, :], C.USS[fc % 2][:, :],
             inc=C.sv)
        C.mark[("a_ev", fc)] = C.sv.v

    for ec in range(DC):
        if ec >= NSD:
            C.wait("g", C.st, C.mark[("dn_mm", ec - NSD)])
        elif ("dn_mm", DC - 1) in C.mark:
            C.wait("g", C.st, C.mark[("dn_mm", DC - 1)])
        C.gslot(4 + ec % NSD, C.DS[ec % NSD][:, :], C.din["dw"][n, ec], ("dn_ld", ec))
        C.gwait("t", ("dn_ld", ec))
        C.wait("t", C.sv, C.mark[("a_ev", FC - 1)])
        C.wait("t", C.sa, C.mark[("sg_ev", FC - 1)])
        if ec >= 2:
            C.wait("t", C.sv, C.mark[("dn_ev", ec - 2)])
        half = 1024 * (ec % 2)
        for fc in range(FC):
            for tt in range(2):
                C.op("t", "matmul", C.PA[:, half + 512 * tt:half + 512 * (tt + 1)],
                     C.DS[ec % NSD][:, 128 * fc:128 * (fc + 1)],
                     C.A_E[fc][:, 512 * tt:512 * (tt + 1)],
                     start=(fc == 0), stop=(fc == FC - 1),
                     inc=(C.st if (tt == 1 and fc == FC - 1) else None))
        C.mark[("dn_mm", ec)] = C.st.v
        C.wait("v", C.st, C.mark[("dn_mm", ec)])
        for tt in range(2):
            C.op("v", "tensor_add",
                 C.B_acc[:, 1024 * ec + 512 * tt:1024 * ec + 512 * (tt + 1)],
                 C.B_acc[:, 1024 * ec + 512 * tt:1024 * ec + 512 * (tt + 1)],
                 C.PA[:, half + 512 * tt:half + 512 * (tt + 1)],
                 inc=(C.sv if tt == 1 else None))
        C.mark[("dn_ev", ec)] = C.sv.v


def _shared_ffn(C, half, fold_store=False):
    NSG = len(C.GUS)
    NSD = len(C.DS)
    for fc in range(FC):
        if fc >= NSG:
            C.wait("g", C.st, C.mark[("gu_mm", fc - NSG)])
        else:
            C.wait("g", C.st, C.mark[("gu_mm", FC - 1)])
        C.gslot(fc % NSG, C.GUS[fc % NSG][:, :], C.din["sguw"][11 * half + fc],
                ("gu_ld", fc))
        C.gwait("t", ("gu_ld", fc))
        if fc >= 1:
            C.wait("t", C.sv, C.mark[("uss_ev", fc - 1)])
            C.wait("t", C.sa, C.mark[("sg_ev", fc - 1)])
        else:
            C.wait("t", C.sv, C.mark[("uss_ev", FC - 1)])
            C.wait("t", C.sa, C.mark[("sg_ev", FC - 1)])
            C.wait("t", C.sv, C.mark[("dn_ev", DC - 1)])
        for dc in range(DC):
            for tt in range(2):
                C.op("t", "matmul", C.PA[:, 512 * tt:512 * (tt + 1)],
                     C.GUS[fc % NSG][:, 128 * dc:128 * (dc + 1)],
                     C.B_xn[:, 1024 * dc + 512 * tt:1024 * dc + 512 * (tt + 1)],
                     start=(dc == 0), stop=(dc == DC - 1))
        for dc in range(DC):
            for tt in range(2):
                C.op("t", "matmul", C.PA[:, 1024 + 512 * tt:1024 + 512 * (tt + 1)],
                     C.GUS[fc % NSG][:, 2048 + 128 * dc:2048 + 128 * (dc + 1)],
                     C.B_xn[:, 1024 * dc + 512 * tt:1024 * dc + 512 * (tt + 1)],
                     start=(dc == 0), stop=(dc == DC - 1),
                     inc=(C.st if (tt == 1 and dc == DC - 1) else None))
        C.mark[("gu_mm", fc)] = C.st.v

        C.wait("a", C.st, C.mark[("gu_mm", fc)])
        if ("a_ev", fc - 2) in C.mark:
            C.wait("a", C.sv, C.mark[("a_ev", fc - 2)])
        C.op("a", "activation", C.SGS[fc % 2][:, :], C.PA[:, 0:1024], AF.Silu, inc=C.sa)
        C.mark[("sg_ev", fc)] = C.sa.v
        C.wait("v", C.st, C.mark[("gu_mm", fc)])
        C.op("v", "tensor_copy", C.USS[fc % 2][:, :], C.PA[:, 1024:2048], inc=C.sv)
        C.mark[("uss_ev", fc)] = C.sv.v
        C.wait("v", C.sa, C.mark[("sg_ev", fc)])
        C.op("v", "tensor_mul", C.A_E[fc][:, :], C.SGS[fc % 2][:, :], C.USS[fc % 2][:, :],
             inc=C.sv)
        C.mark[("a_ev", fc)] = C.sv.v

    for ec in range(DC):
        if ec >= NSD:
            C.wait("g", C.st, C.mark[("dn_mm", ec - NSD)])
        else:
            C.wait("g", C.st, C.mark[("dn_mm", DC - 1)])
        C.gslot(4 + ec % NSD, C.DS[ec % NSD][:, :], C.din["sdw"][half, ec], ("dn_ld", ec))
        C.gwait("t", ("dn_ld", ec))
        C.wait("t", C.sv, C.mark[("a_ev", FC - 1)])
        C.wait("t", C.sa, C.mark[("sg_ev", FC - 1)])
        if ec >= 2:
            C.wait("t", C.sv, C.mark[("dn_ev", ec - 2)])
        hbank = 1024 * (ec % 2)
        for fc in range(FC):
            for tt in range(2):
                C.op("t", "matmul", C.PA[:, hbank + 512 * tt:hbank + 512 * (tt + 1)],
                     C.DS[ec % NSD][:, 128 * fc:128 * (fc + 1)],
                     C.A_E[fc][:, 512 * tt:512 * (tt + 1)],
                     start=(fc == 0), stop=(fc == FC - 1),
                     inc=(C.st if (tt == 1 and fc == FC - 1) else None))
        C.mark[("dn_mm", ec)] = C.st.v
        C.wait("v", C.st, C.mark[("dn_mm", ec)])
        for tt in range(2):
            C.op("v", "tensor_add",
                 C.B_acc[:, 1024 * ec + 512 * tt:1024 * ec + 512 * (tt + 1)],
                 C.B_acc[:, 1024 * ec + 512 * tt:1024 * ec + 512 * (tt + 1)],
                 C.PA[:, hbank + 512 * tt:hbank + 512 * (tt + 1)],
                 inc=(C.sv if tt == 1 else None))
        C.mark[("dn_ev", ec)] = C.sv.v
        if fold_store:
            if ec >= 2:
                C.wait("y", C.sv, C.mark[("p7add", ec - 2)])
            C.op("y", "dma_start", out=C.XST[ec % 2][:, :], in_=C.din["x"][ec],
                 inc=C.sy, incv=16)
            C.mark[("p7ld", ec)] = C.sy.v
            C.wait("v", C.sy, C.mark[("p7ld", ec)])
            C.op("v", "tensor_add", C.B_acc[:, 1024 * ec:1024 * (ec + 1)],
                 C.B_acc[:, 1024 * ec:1024 * (ec + 1)], C.XST[ec % 2][:, :], inc=C.sv)
            C.mark[("p7add", ec)] = C.sv.v
            C.wait("y", C.sv, C.mark[("p7add", ec)])
            C.op("y", "dma_start", out=C.dout[ec],
                 in_=C.B_acc[:, 1024 * ec:1024 * (ec + 1)], inc=C.sy, incv=16)


def _p7_store(C):
    C.wait("y", C.sy)


def _emit(C):
    C.reset()
    _consts_phase(C)
    _p0_layernorm(C)
    _p1_kv(C)
    _a2_m_collective(C)
    _p3_q_g1(C)
    _p3b_consts2_sstart(C)
    _a3_attention(C)
    _a4_wo(C)
    _p4_router(C)
    C.barrier("tvag")
    for n in range(NE):
        _expert_ffn(C, n)
    _shared_ffn(C, 0)
    _shared_ffn(C, 1, fold_store=True)
    _p7_store(C)


def build_nc():
    nc = bass.Bass("TRN2", target_bir_lowering=False)
    C = K()
    C.nc = nc

    din = {}
    din["x"] = nc.dram_tensor("x", [DC, 128, T], F32, kind="ExternalInput")
    for nm, shp in (
        ("wq", [DC, 128, 2048]), ("wk", [DC, 128, 2048]), ("wv", [DC, 128, 2048]),
        ("wo", [DC, 128, 2048]), ("wg1", [128, DC, 128]), ("wg2", [128, 2048]),
        ("gwt", [128, DC, 8]), ("nw", [128, H]),
        ("ldr", [1, H]), ("ldc", [H, 1]),
        ("guw", [NE, FC, 128, 4096]),
        ("dw", [NE, DC, 128, 1408]),
        ("sguw", [FSC, 128, 4096]),
        ("sdw", [2, DC, 128, 1408]),
        ("identf", [128, 128]), ("difft", [128, 128]), ("maskt", [128, 128]),
        ("onesrow", [1, 128]), ("onescol", [128, 1]), ("cposrow", [1, 128]),
        ("revtrow", [1, 128]), ("cdlrow", [1, 128]), ("sel16", [H, 128]),
        ("scmm", [128, 3]), ("scmk", [128, 3]),
    ):
        din[nm] = nc.dram_tensor(nm, shp, F32, kind="ExternalInput")
    C.din = din
    C.dout = nc.dram_tensor("out", [DC, 128, T], F32, kind="ExternalOutput")
    C.dcin = nc.dram_tensor("cin", [H * 128, 128], F32, kind="Internal")
    C.dcout = nc.dram_tensor("cout", [4 * H * 128, 128], F32, kind="Internal")
    C.dwms = nc.dram_tensor("wmsd", [NE, 1024], F32, kind="Internal")

    A = nc.alloc_sbuf_tensor
    C.B_acc = A("B_acc", [128, 16384], F32)
    C.B_xn = A("B_xn", [128, 16384], BF16)
    C.B_q = A("B_q", [128, 16384], BF16)
    C.B_k = A("B_k", [128, 16384], BF16)
    C.B_v = A("B_v", [128, 16384], BF16)
    C.C_cw = A("C_cw", [128, 896], F32)
    C.C_identf = A("C_identf", [128, 128], F32)
    C.C_identb = A("C_identb", [128, 128], BF16)
    C.C_difft = A("C_difft", [128, 128], F32)
    C.C_maskt = A("C_maskt", [128, 128], F32)
    C.C_wg2 = A("C_wg2", [128, 2048], BF16)
    C.C_gwt = A("C_gwt", [128, DC * 8], BF16)
    C.C_nw = A("C_nw", [128, H], F32)
    C.C_ldr = A("C_ldr", [1, H], F32)
    C.C_ldc = A("C_ldc", [H, 1], F32)
    C.C_sel16 = A("C_sel16", [H, 128], F32)
    C.C_onesrow = A("C_onesrow", [1, 128], F32)
    C.C_onescol = A("C_onescol", [128, 1], F32)
    C.C_cposrow = A("C_cposrow", [1, 128], F32)
    C.C_revtrow = A("C_revtrow", [1, 128], F32)
    C.C_cdlrow = A("C_cdlrow", [1, 128], F32)
    C.C_scmm = A("C_scmm", [128, 3], F32)
    C.C_scmk = A("C_scmk", [128, 3], F32)
    C.Rwm = A("Rwm", [128, 64], F32)
    C.Rsc = A("Rsc", [128, 64], F32)

    AT = nc.alloc_sbuf_tensor_at
    ab = nc.lookup_mloc(C.B_acc).addr
    qb = nc.lookup_mloc(C.B_q).addr
    kb = nc.lookup_mloc(C.B_k).addr
    vb = nc.lookup_mloc(C.B_v).addr
    C.VSall = [AT(f"VSall{i}", [128, 2048], BF16, offset=ab + 4096 * i) for i in range(NCH)]
    C.KTMall = [AT(f"KTMall{i}", [128, 2048], BF16, offset=ab + 32768 + 4096 * i)
                for i in range(NCH)]
    C.WSLAB = [AT(f"WSLAB{i}", [128, 2048], BF16, offset=ab + 4096 * i) for i in range(4)]
    C.VST = [AT(f"VST{i}", [128, 1024], BF16, offset=ab + 16384 + 2048 * i) for i in range(2)]
    C.Mst = AT("Mst", [128, 2048], F32, offset=ab + 0)
    C.CDB = AT("CDB", [128, 2048], F32, offset=ab + 0)
    C.G1W = AT("G1W", [128, 2048], BF16, offset=ab + 8192)
    C.G1 = AT("G1", [128, 1024], BF16, offset=ab + 8192)
    C.QS = AT("QS", [128, 2048], BF16, offset=ab + 12288)
    C.S = AT("S", [128, 2048], F32, offset=ab + 16384)
    C.S_bf = AT("S_bf", [128, 2048], BF16, offset=ab + 24576)
    C.DTm = AT("DTm", [128, 2048], F32, offset=ab + 28672)
    C.QDB = AT("QDB", [128, 2048], BF16, offset=ab + 36864)
    C.KTM = AT("KTM", [128, 2048], BF16, offset=ab + 40960)
    C.STm = AT("STm", [128, 2048], BF16, offset=ab + 45056)
    C.SIG = AT("SIG", [128, 2048], BF16, offset=ab + 49152)
    C.OG = AT("OG", [128, 2048], F32, offset=ab + 53248)
    C.OTM = AT("OTM", [128, 2048], BF16, offset=ab + 61440)
    C.Xsq = [AT(f"Xsq{i}", [128, 1024], F32, offset=qb + 4096 * i) for i in range(2)]
    C.Emean = AT("Emean", [1, 1024], F32, offset=qb + 16384)
    C.Evar = AT("Evar", [1, 1024], F32, offset=qb + 20480)
    C.Emrs = AT("Emrs", [1, 1024], F32, offset=qb + 24576)
    C.Brstd = AT("Brstd", [128, 1024], F32, offset=qb + 0)
    C.Bmrs = AT("Bmrs", [128, 1024], F32, offset=qb + 8192)
    C.A_E = [AT(f"A_E{i}", [128, 1024], BF16, offset=qb + 2048 * i) for i in range(FC)]
    C.GUS = [AT(f"GUS{i}", [128, 4096], BF16, offset=kb + 8192 * i) for i in range(3)]
    C.SGS = [AT(f"SGS{i}", [128, 1024], BF16, offset=kb + 24576 + 2048 * i) for i in range(2)]
    C.USS = [AT(f"USS{i}", [128, 1024], BF16, offset=kb + 28672 + 2048 * i) for i in range(2)]
    C.DS = [AT(f"DS{i}", [128, 1408], BF16, offset=qb + 22528 + 2816 * i) for i in range(3)]
    C.WOSLAB = [AT(f"WOSLAB{i}", [128, 2048], BF16, offset=vb + 4096 * i) for i in range(2)]
    C.WMS = AT("WMS", [8, 1024], F32, offset=vb + 8704)
    C.WB = AT("WB", [128, 1024], F32, offset=vb + 12800)
    C.XST = [AT(f"XST{i}", [128, 1024], F32, offset=vb + 16896 + 4096 * i) for i in range(2)]
    C.WMR = AT("WMR", [1, 1024], F32, offset=vb + 28672)

    C.PA = nc.alloc_psum_tensor("PA", [128, 2048], F32)
    C.PB = nc.alloc_psum_tensor("PB", [128, 1024], F32)
    C.PC = nc.alloc_psum_tensor("PC", [128, 2048], BF16)

    C.sgp = [_Sem(nc.alloc_semaphore(name=f"sgp{i}")) for i in range(7)]
    C.sgx = _Sem(nc.alloc_semaphore(name="sgx"))
    C.st = _Sem(nc.alloc_semaphore(name="st"))
    C.sv = _Sem(nc.alloc_semaphore(name="sv"))
    C.sa = _Sem(nc.alloc_semaphore(name="sa"))
    C.sy = _Sem(nc.alloc_semaphore(name="sy"))
    C.sg = _Sem(nc.alloc_semaphore(name="sg"))
    C.scc = _Sem(nc.alloc_semaphore(name="scc"))

    with nc.Block() as block:
        @block.tensor
        def _(e):
            C.cur, C.h = "t", e
            _emit(C)

        @block.vector
        def _(e):
            C.cur, C.h = "v", e
            _emit(C)

        @block.scalar
        def _(e):
            C.cur, C.h = "a", e
            _emit(C)

        @block.gpsimd
        def _(e):
            C.cur, C.h = "g", e
            _emit(C)

        @block.sync
        def _(e):
            C.cur, C.h = "y", e
            _emit(C)

    return nc



def _lhsT_slabs(W, kc, mc):
    A = W.reshape(kc, 128, mc, 128).transpose(2, 1, 0, 3)
    return np.ascontiguousarray(A).reshape(mc, 128, kc * 128)


def _prep_shared(inputs):
    f = np.float32
    d = {}
    d["wq"] = _lhsT_slabs(np.asarray(inputs["L0_attn_Wq"], f), DC, DC)
    d["wk"] = _lhsT_slabs(np.asarray(inputs["L0_attn_Wk"], f), DC, DC)
    d["wv"] = _lhsT_slabs(np.asarray(inputs["L0_attn_Wv"], f), DC, DC)
    d["wo"] = _lhsT_slabs(np.asarray(inputs["L0_attn_Wo"], f), DC, DC)
    d["wg1"] = np.ascontiguousarray(
        np.asarray(inputs["L0_attn_Wg1"], f).reshape(DC, 128, 128).transpose(1, 0, 2))
    d["wg2"] = np.ascontiguousarray(np.asarray(inputs["L0_attn_Wg2"], f))
    d["gwt"] = np.ascontiguousarray(
        np.asarray(inputs["L0_ffn_gate_weight"], f).T.reshape(DC, 128, NE).transpose(1, 0, 2))
    d["nw"] = np.ascontiguousarray(np.asarray(inputs["L0_attn_norm_w"], f).reshape(H, 128).T)
    ld = np.asarray(inputs["L0_attn_log_decay"], f)
    d["ldr"] = ld.reshape(1, H).copy()
    d["ldc"] = ld.reshape(H, 1).copy()
    gw = np.asarray(inputs["L0_ffn_gate_w"], f)
    uw = np.asarray(inputs["L0_ffn_up_w"], f)
    dw = np.asarray(inputs["L0_ffn_down_w"], f)
    gws = np.stack([_lhsT_slabs(gw[n].T, DC, FC) for n in range(NE)])
    uws = np.stack([_lhsT_slabs(uw[n].T, DC, FC) for n in range(NE)])
    d["guw"] = np.ascontiguousarray(np.concatenate([gws, uws], axis=-1))
    d["dw"] = np.stack([_lhsT_slabs(dw[n].T, FC, DC) for n in range(NE)])
    sgws = _lhsT_slabs(np.asarray(inputs["L0_ffn_sh_gate_w"], f).T, DC, FSC)
    suws = _lhsT_slabs(np.asarray(inputs["L0_ffn_sh_up_w"], f).T, DC, FSC)
    d["sguw"] = np.ascontiguousarray(np.concatenate([sgws, suws], axis=-1))
    sd = np.asarray(inputs["L0_ffn_sh_down_w"], f).T.reshape(FSC, 128, DC, 128)
    sdf = sd.transpose(2, 1, 0, 3)
    d["sdw"] = np.ascontiguousarray(
        np.stack([sdf[:, :, 0:FC, :], sdf[:, :, FC:FSC, :]])).reshape(2, DC, 128, FC * 128)
    idx = np.arange(128, dtype=f)
    d["identf"] = np.eye(128, dtype=f)
    d["difft"] = np.maximum(idx[None, :] - idx[:, None], 0.0).astype(f)
    d["maskt"] = (idx[None, :] >= idx[:, None]).astype(f)
    d["onesrow"] = np.ones((1, 128), f)
    d["onescol"] = np.ones((128, 1), f)
    d["cposrow"] = (idx + 1.0).reshape(1, 128).astype(f)
    d["revtrow"] = (127.0 - idx).reshape(1, 128).astype(f)
    d["cdlrow"] = np.tile(128.0 * (7.0 - np.arange(8, dtype=f)), H).reshape(1, 128)
    sel = np.zeros((H, 128), f)
    for h in range(H):
        sel[h, 8 * h:8 * (h + 1)] = 1.0
    d["sel16"] = sel
    return d


def kernel(**inputs):
    shared = _prep_shared(inputs)
    x = np.asarray(inputs["L0_attn_x"], np.float32).reshape(NCORE * T, D)
    nc = build_nc()
    in_maps = []
    for c in range(NCORE):
        m = dict(shared)
        shard = x[T * c:T * (c + 1)].reshape(T, DC, 128).transpose(1, 2, 0)
        m["x"] = np.ascontiguousarray(shard)
        j = c % 4
        mult = np.array([j - 1 - jp for jp in range(3)], np.float32)
        msk = np.array([1.0 if jp < j else 0.0 for jp in range(3)], np.float32)
        m["scmm"] = np.tile(1024.0 * mult * msk, (128, 1)).astype(np.float32)
        m["scmk"] = np.tile(msk, (128, 1)).astype(np.float32)
        in_maps.append(m)
    res = run_bass_kernel_spmd(nc, in_maps, core_ids=list(range(NCORE)))
    shards = []
    for c in range(NCORE):
        o = res.results[c]["out"]
        shards.append(np.ascontiguousarray(np.transpose(o, (2, 0, 1))).reshape(T, D))
    return np.concatenate(shards, axis=0).reshape(2, 4096, D).astype(np.float32)


if __name__ == "__main__":
    nc = build_nc()
    print("instructions:", nc.n_instructions())
    print("sbuf remaining:", nc.sbuf_bytes_remaining)


# version 09
# speedup vs baseline: 1.0203x; 1.0203x over previous
import dataclasses

import numpy as np

import concourse.bass as bass
import concourse.mybir as mybir
from concourse.bass_utils import run_bass_kernel_spmd

F32 = mybir.dt.float32
BF16 = mybir.dt.bfloat16
AF = mybir.ActivationFunctionType
ALU = mybir.AluOpType
AX = mybir.AxisListType.X

NCORE = 8
D = 2048
T = 1024
DC = 16
NCH = 8
H = 16
NE = 8
FC = 11
FSC = 22
EPS = 1e-5
BIG = 1e30


class _Sem:
    __slots__ = ("h", "v")

    def __init__(self, h):
        self.h = h
        self.v = 0


class K:

    def reset(self):
        for s in (self.st, self.sv, self.sa, self.sy, self.sg, self.scc,
                  self.sgx, *self.sgp):
            s.v = 0
        self.mark = {}

    def op(self, ek, fname, *args, inc=None, incv=1, **kw):
        if ek in ("v", "a"):
            s = self.sv if ek == "v" else self.sa
            assert inc is None or inc is s
            pre = s.v
            s.v += 1
            if self.cur == ek:
                self.h.wait_ge(s.h, pre)
                getattr(self.h, fname)(*args, **kw).then_inc(s.h, 1)
            return
        if inc is not None:
            inc.v += incv
        if self.cur == ek:
            inst = getattr(self.h, fname)(*args, **kw)
            if inc is not None:
                inst.then_inc(inc.h, incv)

    def wait(self, ek, sem, v=None):
        if self.cur == ek:
            self.h.wait_ge(sem.h, sem.v if v is None else v)

    def gslot(self, slot, out, in_, key):
        self.op("g", "dma_start", out=out, in_=in_, inc=self.sgp[slot], incv=16)
        self.mark[key] = (slot, self.sgp[slot].v)

    def gwait(self, ek, key):
        slot, v = self.mark[key]
        self.wait(ek, self.sgp[slot], v)

    def barrier(self, engines="tva"):
        for ek in engines:
            for s in (self.st, self.sv, self.sa, self.sy, self.sg,
                      self.sgx, *self.sgp):
                if s.v > 0:
                    self.wait(ek, s)



def _consts_phase(C):
    cw = C.C_cw
    for name, dst in (
        ("identf", C.C_identf), ("difft", C.C_difft), ("maskt", C.C_maskt),
        ("ldr", C.C_ldr), ("sel16", C.C_sel16), ("ldc", C.C_ldc),
        ("onesrow", C.C_onesrow), ("onescol", C.C_onescol),
        ("cposrow", C.C_cposrow), ("revtrow", C.C_revtrow),
        ("cdlrow", C.C_cdlrow), ("scmm", C.C_scmm), ("scmk", C.C_scmk),
        ("nw", C.C_nw),
    ):
        sl = tuple(slice(0, s) for s in dst.shape)
        C.op("y", "dma_start", out=dst[sl], in_=C.din[name][sl], inc=C.sy, incv=16)
    C.op("g", "dma_start", out=C.C_identb[:, :], in_=C.din["identf"][:, :], inc=C.sg, incv=16)
    C.op("g", "dma_start", out=C.C_wg2[:, :], in_=C.din["wg2"][:, :], inc=C.sg, incv=16)
    C.op("g", "dma_start", out=C.C_gwt[:, :],
         in_=C.din["gwt"].ap().rearrange("p a b -> p (a b)"), inc=C.sg, incv=16)

    C.wait("t", C.sy)
    C.op("t", "matmul", C.PA[:, 0:16], C.C_onesrow[0:1, :], C.C_ldr[0:1, :],
         start=True, stop=True)
    C.op("t", "matmul", C.PA[:, 16:32], C.C_revtrow[0:1, :], C.C_ldr[0:1, :],
         start=True, stop=True)
    C.op("t", "matmul", C.PA[0:1, 48:176], C.C_ldc[0:H, 0:1], C.C_sel16[0:H, :],
         start=True, stop=True, inc=C.st)
    C.wait("a", C.st)
    C.op("a", "activation", cw[:, 0:16], C.PA[:, 0:16], AF.Copy)
    C.op("a", "activation", cw[:, 16:32], C.PA[:, 16:32], AF.Exp)
    C.op("a", "activation", cw[0:1, 192:320], C.PA[0:1, 48:176], AF.Copy, inc=C.sa)
    C.wait("v", C.sa)
    C.op("v", "tensor_scalar", cw[0:1, 320:336], C.C_ldr[0:1, :], 128.0, None, ALU.mult)
    C.op("v", "tensor_mul", cw[0:1, 336:464], cw[0:1, 192:320], C.C_cdlrow[0:1, :], inc=C.sv)
    C.wait("a", C.sv)
    C.op("a", "activation", cw[0:1, 464:480], cw[0:1, 320:336], AF.Exp)
    C.op("a", "activation", cw[0:1, 480:608], cw[0:1, 336:464], AF.Exp, inc=C.sa)
    C.wait("t", C.sa)
    C.op("t", "matmul", C.PA[:, 48:176], C.C_onesrow[0:1, :], cw[0:1, 480:608],
         start=True, stop=True)
    C.op("t", "matmul", C.PA[:, 176:192], C.C_onesrow[0:1, :], cw[0:1, 464:480],
         start=True, stop=True, inc=C.st)
    C.wait("v", C.st)
    C.op("v", "tensor_copy", cw[:, 64:192], C.PA[:, 48:176])
    C.op("v", "tensor_copy", cw[:, 48:64], C.PA[:, 176:192], inc=C.sv)


def _p0_layernorm(C):
    for dc in range(DC):
        C.op("y", "dma_start", out=C.B_acc[:, 1024 * dc:1024 * (dc + 1)],
             in_=C.din["x"][dc], inc=C.sy, incv=16)
        C.mark[("xld", dc)] = C.sy.v
    for dc in range(DC):
        C.wait("a", C.sy, C.mark[("xld", dc)])
        if dc >= 2:
            C.wait("a", C.st, C.mark[("p0mm", dc - 2)])
        C.op("a", "activation", C.Xsq[dc % 2][:, :], C.B_acc[:, 1024 * dc:1024 * (dc + 1)],
             AF.Square, inc=C.sa)
        C.mark[("p0sq", dc)] = C.sa.v
        C.wait("t", C.sa, C.mark[("p0sq", dc)])
        C.wait("t", C.sy, C.mark[("xld", dc)])
        for tt in range(2):
            C.op("t", "matmul", C.PB[0:1, 512 * tt:512 * (tt + 1)],
                 C.C_onescol[:, 0:1],
                 C.B_acc[:, 1024 * dc + 512 * tt:1024 * dc + 512 * (tt + 1)],
                 start=(dc == 0), stop=(dc == DC - 1))
            C.op("t", "matmul", C.PA[0:1, 512 * tt:512 * (tt + 1)],
                 C.C_onescol[:, 0:1], C.Xsq[dc % 2][:, 512 * tt:512 * (tt + 1)],
                 start=(dc == 0), stop=(dc == DC - 1),
                 inc=(C.st if tt == 1 else None))
        C.mark[("p0mm", dc)] = C.st.v
    C.wait("v", C.st)
    C.op("v", "tensor_scalar", C.Emean[0:1, :], C.PB[0:1, 0:1024], 1.0 / D, None, ALU.mult)
    C.op("v", "tensor_scalar", C.Evar[0:1, :], C.PA[0:1, 0:1024], 1.0 / D, None, ALU.mult)
    C.op("v", "tensor_mul", C.Emrs[0:1, :], C.Emean[0:1, :], C.Emean[0:1, :])
    C.op("v", "tensor_sub", C.Evar[0:1, :], C.Evar[0:1, :], C.Emrs[0:1, :], inc=C.sv)
    C.op("v", "tensor_scalar", C.Evar[0:1, :], C.Evar[0:1, :], EPS, None, ALU.add, inc=C.sv)
    C.wait("a", C.sv)
    C.op("a", "activation", C.Evar[0:1, :], C.Evar[0:1, :], AF.Sqrt, inc=C.sa)
    C.wait("v", C.sa)
    C.op("v", "reciprocal", C.Evar[0:1, :], C.Evar[0:1, :])
    C.op("v", "tensor_mul", C.Emrs[0:1, :], C.Emean[0:1, :], C.Evar[0:1, :], inc=C.sv)
    C.wait("t", C.sv)
    for tt in range(2):
        C.op("t", "matmul", C.PA[:, 512 * tt:512 * (tt + 1)], C.C_onesrow[0:1, :],
             C.Evar[0:1, 512 * tt:512 * (tt + 1)], start=True, stop=True)
        C.op("t", "matmul", C.PA[:, 1024 + 512 * tt:1024 + 512 * (tt + 1)],
             C.C_onesrow[0:1, :], C.Emrs[0:1, 512 * tt:512 * (tt + 1)],
             start=True, stop=True, inc=(C.st if tt == 1 else None))
    C.wait("v", C.st)
    C.op("v", "tensor_copy", C.Brstd[:, :], C.PA[:, 0:1024])
    C.op("v", "tensor_copy", C.Bmrs[:, :], C.PA[:, 1024:2048], inc=C.sv)
    for dc in range(DC):
        C.op("v", "tensor_mul", C.Xsq[1][:, :], C.B_acc[:, 1024 * dc:1024 * (dc + 1)],
             C.Brstd[:, :])
        C.op("v", "tensor_sub", C.B_xn[:, 1024 * dc:1024 * (dc + 1)], C.Xsq[1][:, :],
             C.Bmrs[:, :], inc=(C.sv if dc == DC - 1 else None))
    C.mark["xn"] = C.sv.v


def _projection(C, wname, dst, pre_evac_wait=None, post=None):
    C.barrier("g")
    C.barrier("t")
    NS = len(C.WSLAB)
    for ec in range(DC):
        if ec >= NS:
            C.wait("g", C.st, C.mark[("proj_mm", wname, ec - NS)])
        C.gslot(ec % NS, C.WSLAB[ec % NS][:, :], C.din[wname][ec],
                ("proj_ld", wname, ec))
        C.gwait("t", ("proj_ld", wname, ec))
        if ec >= 2:
            C.wait("t", C.sa, C.mark[("proj_ev", wname, ec - 2)])
        for dc in range(DC):
            for tt in range(2):
                bank = 1024 * (ec % 2) + 512 * tt
                C.op("t", "matmul", C.PA[:, bank:bank + 512],
                     C.WSLAB[ec % NS][:, 128 * dc:128 * (dc + 1)],
                     C.B_xn[:, 1024 * dc + 512 * tt:1024 * dc + 512 * (tt + 1)],
                     start=(dc == 0), stop=(dc == DC - 1),
                     inc=(C.st if (tt == 1 and dc == DC - 1) else None))
        C.mark[("proj_mm", wname, ec)] = C.st.v

        C.wait("a", C.st, C.mark[("proj_mm", wname, ec)])
        if pre_evac_wait is not None:
            pre_evac_wait(ec)
        half = 1024 * (ec % 2)
        C.op("a", "activation", dst(ec)[:, 0:512], C.PA[:, half:half + 512], AF.Silu)
        C.op("a", "activation", dst(ec)[:, 512:1024], C.PA[:, half + 512:half + 1024],
             AF.Silu, inc=C.sa)
        C.mark[("proj_ev", wname, ec)] = C.sa.v
        if post is not None:
            post(ec)


def _p1_kv(C):
    _projection(C, "wk", lambda ec: C.B_k[:, 1024 * ec:1024 * (ec + 1)])

    def v_pre_wait(ec):
        if ec >= 2:
            C.wait("a", C.st, C.mark[("vtr", ec - 2)])

    def v_post(ec):
        C.wait("t", C.sa, C.mark[("proj_ev", "wv", ec)])
        if ec >= 2:
            C.wait("t", C.sv, C.mark[("vtr_ev", ec - 2)])
        half = 1024 * (ec % 2)
        for ch in range(NCH):
            C.op("t", "transpose", C.PC[:, half + 128 * ch:half + 128 * (ch + 1)],
                 C.VST[ec % 2][:, 128 * ch:128 * (ch + 1)], C.C_identb[:, :],
                 inc=(C.st if ch == NCH - 1 else None))
        C.mark[("vtr", ec)] = C.st.v
        C.wait("v", C.st, C.mark[("vtr", ec)])
        for ch in range(NCH):
            C.op("v", "tensor_copy",
                 C.B_v[:, 2048 * ch + 128 * ec:2048 * ch + 128 * (ec + 1)],
                 C.PC[:, half + 128 * ch:half + 128 * (ch + 1)],
                 inc=(C.sv if ch == NCH - 1 else None))
        C.mark[("vtr_ev", ec)] = C.sv.v

    _projection(C, "wv", lambda ec: C.VST[ec % 2], pre_evac_wait=v_pre_wait, post=v_post)
    C.barrier("tva")


def _a2_m_collective(C):
    for ch in range(NCH):
        if ch >= 1:
            C.wait("t", C.sv, C.mark[("a2ev", ch - 1)])
        for h in range(H):
            C.op("t", "transpose", C.PC[:, 128 * h:128 * (h + 1)],
                 C.B_k[:, 1024 * h + 128 * ch:1024 * h + 128 * (ch + 1)],
                 C.C_identb[:, :], inc=(C.st if h == H - 1 else None))
        C.mark[("a2tr", ch)] = C.st.v
        C.wait("v", C.st, C.mark[("a2tr", ch)])
        for h in range(H):
            C.op("v", "tensor_scalar", C.KTMall[ch][:, 128 * h:128 * (h + 1)],
                 C.PC[:, 128 * h:128 * (h + 1)], C.C_cw[:, 16 + h:17 + h], None, ALU.mult)
            C.op("v", "tensor_scalar", C.VSall[ch][:, 128 * h:128 * (h + 1)],
                 C.B_v[:, 2048 * ch + 128 * h:2048 * ch + 128 * (h + 1)],
                 C.C_cw[:, 64 + 8 * h + ch:64 + 8 * h + ch + 1], None, ALU.mult,
                 inc=(C.sv if h == H - 1 else None))
        C.mark[("a2ev", ch)] = C.sv.v
    C.wait("t", C.sv)
    for h in range(H):
        for ch in range(NCH):
            C.op("t", "matmul", C.PA[:, 128 * h:128 * (h + 1)],
                 C.KTMall[ch][:, 128 * h:128 * (h + 1)],
                 C.VSall[ch][:, 128 * h:128 * (h + 1)],
                 start=(ch == 0), stop=(ch == NCH - 1),
                 inc=(C.st if (h == H - 1 and ch == NCH - 1) else None))
    C.wait("v", C.st)
    C.op("v", "tensor_copy", C.Mst[:, :], C.PA[:, :], inc=C.sv)
    C.mark["m_done"] = C.sv.v
    C.wait("y", C.sv, C.mark["m_done"])
    C.op("y", "dma_start", out=C.dcin.ap().rearrange("(h p) e -> p h e", p=128),
         in_=C.Mst[:, :].rearrange("p (h e) -> p h e", h=H), inc=C.sy, incv=16)
    C.mark["cin"] = C.sy.v
    C.wait("g", C.sy, C.mark["cin"])
    C.op("g", "collective_compute", "AllGather", ALU.bypass,
         replica_groups=[[0, 1, 2, 3], [4, 5, 6, 7]],
         ins=[C.dcin.ap().opt()], outs=[C.dcout.ap().opt()], inc=C.scc)
    C.mark["cc"] = C.scc.v


def _p3_q_g1(C):
    _projection(C, "wq", lambda ec: C.B_q[:, 1024 * ec:1024 * (ec + 1)])
    C.wait("g", C.st, C.mark[("proj_mm", "wq", DC - 1)])
    C.op("g", "dma_start", out=C.G1W[:, :],
         in_=C.din["wg1"].ap().rearrange("p a b -> p (a b)"), inc=C.sgx, incv=16)
    C.barrier("t")
    C.wait("t", C.sgx)
    for dc in range(DC):
        for tt in range(2):
            C.op("t", "matmul", C.PB[:, 512 * tt:512 * (tt + 1)],
                 C.G1W[:, 128 * dc:128 * (dc + 1)],
                 C.B_xn[:, 1024 * dc + 512 * tt:1024 * dc + 512 * (tt + 1)],
                 start=(dc == 0), stop=(dc == DC - 1),
                 inc=(C.st if (tt == 1 and dc == DC - 1) else None))
    C.wait("v", C.st)
    C.op("v", "tensor_copy", C.G1[:, :], C.PB[:, 0:1024], inc=C.sv)
    C.mark["g1"] = C.sv.v


def _p3b_consts2_sstart(C):
    cw = C.C_cw
    C.barrier("tvay")
    for h in range(H):
        C.op("v", "tensor_scalar", cw[0:1, 336:464], C.C_cposrow[0:1, :],
             C.C_ldr[0:1, h:h + 1], None, ALU.mult, inc=C.sv)
        C.mark[("qr1", h)] = C.sv.v
        C.wait("a", C.sv, C.mark[("qr1", h)])
        C.op("a", "activation", cw[0:1, 464:592], cw[0:1, 336:464], AF.Exp, inc=C.sa)
        C.mark[("qr2", h)] = C.sa.v
        C.wait("t", C.sa, C.mark[("qr2", h)])
        if h >= 4:
            C.wait("t", C.sv, C.mark[("qdbev", h - 4)])
        C.op("t", "matmul", C.PA[:, 128 * (h % 4):128 * (h % 4 + 1)],
             C.C_onesrow[0:1, :], cw[0:1, 464:592], start=True, stop=True, inc=C.st)
        C.mark[("qdbmm", h)] = C.st.v
        C.wait("v", C.st, C.mark[("qdbmm", h)])
        C.op("v", "tensor_copy", C.QDB[:, 128 * h:128 * (h + 1)],
             C.PA[:, 128 * (h % 4):128 * (h % 4 + 1)], inc=C.sv)
        C.mark[("qdbev", h)] = C.sv.v
    for h in range(H):
        C.op("v", "tensor_scalar", C.OG[:, 0:128], C.C_difft[:, :], cw[:, h:h + 1],
             None, ALU.mult, inc=C.sv)
        C.mark[("dtm1", h)] = C.sv.v
        C.wait("a", C.sv, C.mark[("dtm1", h)])
        C.op("a", "activation", C.OG[:, 128:256], C.OG[:, 0:128], AF.Exp, inc=C.sa)
        C.mark[("dtm2", h)] = C.sa.v
        C.wait("v", C.sa, C.mark[("dtm2", h)])
        C.op("v", "tensor_mul", C.DTm[:, 128 * h:128 * (h + 1)], C.OG[:, 128:256],
             C.C_maskt[:, :], inc=C.sv)
    for jp in range(3):
        C.op("v", "tensor_scalar", cw[:, 832:848], cw[:, 0:16], C.C_scmm[:, jp:jp + 1],
             None, ALU.mult, inc=C.sv)
        C.mark[("cf1", jp)] = C.sv.v
        C.wait("a", C.sv, C.mark[("cf1", jp)])
        C.op("a", "activation", cw[:, 848:864], cw[:, 832:848], AF.Exp, inc=C.sa)
        C.mark[("cf2", jp)] = C.sa.v
        C.wait("v", C.sa, C.mark[("cf2", jp)])
        C.op("v", "tensor_scalar", cw[:, 736 + 16 * jp:752 + 16 * jp], cw[:, 848:864],
             C.C_scmk[:, jp:jp + 1], None, ALU.mult, inc=C.sv)
    C.mark["coef"] = C.sv.v
    C.wait("y", C.scc, C.mark["cc"])
    for jp in range(3):
        if jp >= 1:
            C.wait("y", C.sv, C.mark[("sacc", jp - 1)])
        C.op("y", "dma_start",
             out=C.Mst[:, :].rearrange("p (h e) -> p h e", h=H),
             in_=C.dcout.ap()[2048 * jp:2048 * (jp + 1), :].rearrange(
                 "(h p) e -> p h e", p=128),
             inc=C.sy, incv=16)
        C.mark[("smld", jp)] = C.sy.v
        C.wait("v", C.sy, C.mark[("smld", jp)])
        for h in range(H):
            if jp == 0:
                C.op("v", "tensor_scalar", C.S[:, 128 * h:128 * (h + 1)],
                     C.Mst[:, 128 * h:128 * (h + 1)], cw[:, 736 + h:737 + h],
                     None, ALU.mult, inc=(C.sv if h == H - 1 else None))
            else:
                C.op("v", "tensor_scalar", C.OG[:, 0:128],
                     C.Mst[:, 128 * h:128 * (h + 1)],
                     cw[:, 736 + 16 * jp + h:737 + 16 * jp + h], None, ALU.mult)
                C.op("v", "tensor_add", C.S[:, 128 * h:128 * (h + 1)],
                     C.S[:, 128 * h:128 * (h + 1)], C.OG[:, 0:128],
                     inc=(C.sv if h == H - 1 else None))
        C.mark[("sacc", jp)] = C.sv.v
    C.op("v", "tensor_copy", C.S_bf[:, :], C.S[:, :], inc=C.sv)
    C.mark["sbf"] = C.sv.v
    C.op("v", "tensor_scalar", cw[0:1, 320:336], C.C_ldr[0:1, :], 128.0, None,
         ALU.mult, inc=C.sv)
    C.mark["cdr0"] = C.sv.v
    C.wait("a", C.sv, C.mark["cdr0"])
    C.op("a", "activation", cw[0:1, 592:608], cw[0:1, 320:336], AF.Exp, inc=C.sa)
    C.mark["cdr1"] = C.sa.v
    C.wait("v", C.sa, C.mark["cdr1"])
    for h in range(H):
        C.op("v", "tensor_scalar", cw[0:1, 336:464], C.C_onesrow[0:1, :],
             cw[0:1, 592 + h:593 + h], None, ALU.mult, inc=C.sv)
        C.mark[("cdr", h)] = C.sv.v
        C.wait("t", C.sv, C.mark[("cdr", h)])
        if h >= 4:
            C.wait("t", C.sv, C.mark[("cdbe", h - 4)])
        C.op("t", "matmul", C.PA[:, 128 * (h % 4):128 * (h % 4 + 1)],
             C.C_onesrow[0:1, :], cw[0:1, 336:464], start=True, stop=True, inc=C.st)
        C.mark[("cdbm", h)] = C.st.v
        C.wait("v", C.st, C.mark[("cdbm", h)])
        C.op("v", "tensor_copy", C.CDB[:, 128 * h:128 * (h + 1)],
             C.PA[:, 128 * (h % 4):128 * (h % 4 + 1)], inc=C.sv)
        C.mark[("cdbe", h)] = C.sv.v


def _a3_attention(C):
    cw = C.C_cw
    C.barrier("t")
    C.mark["pa_free"] = 0
    C.mark["pb_free"] = 0
    C.mark["pc_free"] = 0
    for ch in range(NCH):
        C.wait("t", C.sv, C.mark["pa_free"])
        for et in range(4):
            C.op("t", "matmul", C.PA[:, 512 * et:512 * (et + 1)],
                 C.G1[:, 128 * ch:128 * (ch + 1)],
                 C.C_wg2[:, 512 * et:512 * (et + 1)], start=True, stop=True,
                 inc=(C.st if et == 3 else None))
        C.mark["gate_mm"] = C.st.v
        C.wait("a", C.st, C.mark["gate_mm"])
        C.op("a", "activation", C.SIG[:, :], C.PA[:, :], AF.Sigmoid, inc=C.sa)
        C.mark["sig"] = C.sa.v
        C.wait("t", C.sv, C.mark["pc_free"])
        for h in range(H):
            C.op("t", "transpose", C.PC[:, 128 * h:128 * (h + 1)],
                 C.B_k[:, 1024 * h + 128 * ch:1024 * h + 128 * (ch + 1)],
                 C.C_identb[:, :], inc=(C.st if h == H - 1 else None))
        C.mark["ktr"] = C.st.v
        C.wait("v", C.st, C.mark["ktr"])
        for h in range(H):
            C.op("v", "tensor_scalar", C.KTM[:, 128 * h:128 * (h + 1)],
                 C.PC[:, 128 * h:128 * (h + 1)], cw[:, 16 + h:17 + h], None, ALU.mult,
                 inc=(C.sv if h == H - 1 else None))
        C.mark["ktm"] = C.sv.v
        C.wait("t", C.sv, C.mark["pb_free"])
        for w in range(2):
            if w == 1:
                C.wait("t", C.sv, C.mark[("scev", 0)])
            for i in range(8):
                h = 8 * w + i
                C.op("t", "matmul", C.PB[:, 128 * i:128 * (i + 1)],
                     C.B_k[:, 1024 * h + 128 * ch:1024 * h + 128 * (ch + 1)],
                     C.B_q[:, 1024 * h + 128 * ch:1024 * h + 128 * (ch + 1)],
                     start=True, stop=True, inc=(C.st if i == 7 else None))
            C.mark[("sc", w)] = C.st.v
            C.wait("v", C.st, C.mark[("sc", w)])
            C.op("v", "tensor_mul", C.STm[:, 1024 * w:1024 * (w + 1)],
                 C.PB[:, 0:1024], C.DTm[:, 1024 * w:1024 * (w + 1)], inc=C.sv)
            C.mark[("scev", w)] = C.sv.v
        C.op("v", "tensor_mul",
             C.QS[:, :].rearrange("p (h e) -> p h e", h=H),
             C.B_q[:, :].rearrange("p (h t) -> p h t", h=H)[:, :, 128 * ch:128 * (ch + 1)],
             C.QDB[:, :].rearrange("p (h e) -> p h e", h=H), inc=C.sv)
        C.mark["qs"] = C.sv.v
        C.wait("t", C.sv, C.mark["qs"])
        C.wait("t", C.sa, C.mark["sig"])
        for h in range(H):
            C.op("t", "matmul", C.PA[:, 128 * h:128 * (h + 1)],
                 C.STm[:, 128 * h:128 * (h + 1)],
                 C.B_v[:, 2048 * ch + 128 * h:2048 * ch + 128 * (h + 1)],
                 start=True, stop=False)
            C.op("t", "matmul", C.PA[:, 128 * h:128 * (h + 1)],
                 C.QS[:, 128 * h:128 * (h + 1)],
                 C.S_bf[:, 128 * h:128 * (h + 1)],
                 start=False, stop=True, inc=(C.st if h == H - 1 else None))
        C.mark["omm"] = C.st.v
        C.wait("v", C.st, C.mark["omm"])
        C.op("v", "tensor_mul", C.OG[:, :], C.PA[:, :], C.SIG[:, :], inc=C.sv)
        C.mark["og"] = C.sv.v
        if ch < NCH - 1:
            for w in range(2):
                C.wait("t", C.sv, C.mark[("scev", 1)] if w == 0 else C.mark[("sup", 0)])
                for i in range(8):
                    h = 8 * w + i
                    C.op("t", "matmul", C.PB[:, 128 * i:128 * (i + 1)],
                         C.KTM[:, 128 * h:128 * (h + 1)],
                         C.B_v[:, 2048 * ch + 128 * h:2048 * ch + 128 * (h + 1)],
                         start=True, stop=True, inc=(C.st if i == 7 else None))
                C.mark[("kv", w)] = C.st.v
                C.wait("v", C.st, C.mark[("kv", w)])
                C.op("v", "tensor_mul", C.S[:, 1024 * w:1024 * (w + 1)],
                     C.S[:, 1024 * w:1024 * (w + 1)], C.CDB[:, 1024 * w:1024 * (w + 1)])
                C.op("v", "tensor_add", C.S[:, 1024 * w:1024 * (w + 1)],
                     C.S[:, 1024 * w:1024 * (w + 1)], C.PB[:, 0:1024], inc=C.sv)
                C.mark[("sup", w)] = C.sv.v
            C.mark["pb_free"] = C.mark[("sup", 1)]
            C.op("v", "tensor_copy", C.S_bf[:, :], C.S[:, :], inc=C.sv)
            C.mark["sbf"] = C.sv.v
        else:
            C.mark["pb_free"] = C.mark[("scev", 1)]
        C.wait("a", C.sv, C.mark["og"])
        for h in range(H):
            C.op("a", "activation", C.OTM[:, 128 * h:128 * (h + 1)],
                 C.OG[:, 128 * h:128 * (h + 1)], AF.Square,
                 accum_out=cw[:, 864 + h:865 + h],
                 inc=(C.sa if h == H - 1 else None))
        C.mark["sq"] = C.sa.v
        C.mark["pa_free"] = C.mark["og"]
        C.wait("v", C.sa, C.mark["sq"])
        C.op("v", "tensor_scalar", cw[:, 880:896], cw[:, 864:880], 1.0 / 128.0, EPS,
             ALU.mult, ALU.add, inc=C.sv)
        C.mark["rsq"] = C.sv.v
        C.wait("a", C.sv, C.mark["rsq"])
        C.op("a", "activation", cw[:, 880:896], cw[:, 880:896], AF.Sqrt,
             inc=C.sa)
        C.mark["rsqrt"] = C.sa.v
        C.wait("v", C.sa, C.mark["rsqrt"])
        C.op("v", "reciprocal", cw[:, 880:896], cw[:, 880:896])
        for h in range(H):
            C.op("v", "tensor_scalar", C.OTM[:, 128 * h:128 * (h + 1)],
                 C.OG[:, 128 * h:128 * (h + 1)], cw[:, 880 + h:881 + h], None, ALU.mult,
                 inc=(C.sv if h == H - 1 else None))
        C.mark["otm"] = C.sv.v
        C.wait("t", C.sv, C.mark["otm"])
        C.wait("t", C.sv, C.mark["ktm"])
        for h in range(H):
            C.op("t", "transpose", C.PC[:, 128 * h:128 * (h + 1)],
                 C.OTM[:, 128 * h:128 * (h + 1)], C.C_identb[:, :],
                 inc=(C.st if h == H - 1 else None))
        C.mark["otr"] = C.st.v
        C.wait("v", C.st, C.mark["otr"])
        for h in range(H):
            C.op("v", "tensor_scalar",
                 C.B_q[:, 1024 * h + 128 * ch:1024 * h + 128 * (ch + 1)],
                 C.PC[:, 128 * h:128 * (h + 1)], C.C_nw[:, h:h + 1], None, ALU.mult,
                 inc=(C.sv if h == H - 1 else None))
        C.mark["pc_free"] = C.sv.v


def _a4_wo(C):
    C.barrier("tva")
    C.barrier("g")
    NS = len(C.WOSLAB)
    for ec in range(DC):
        if ec >= NS:
            C.wait("g", C.st, C.mark[("wo_mm", ec - NS)])
        C.gslot(ec % NS, C.WOSLAB[ec % NS][:, :], C.din["wo"][ec], ("wo_ld", ec))
        C.gwait("t", ("wo_ld", ec))
        if ec >= 2:
            C.wait("t", C.sv, C.mark[("wo_ev", ec - 2)])
        for dc in range(DC):
            for tt in range(2):
                bank = 1024 * (ec % 2) + 512 * tt
                C.op("t", "matmul", C.PA[:, bank:bank + 512],
                     C.WOSLAB[ec % NS][:, 128 * dc:128 * (dc + 1)],
                     C.B_q[:, 1024 * dc + 512 * tt:1024 * dc + 512 * (tt + 1)],
                     start=(dc == 0), stop=(dc == DC - 1),
                     inc=(C.st if (tt == 1 and dc == DC - 1) else None))
        C.mark[("wo_mm", ec)] = C.st.v
        C.wait("v", C.st, C.mark[("wo_mm", ec)])
        half = 1024 * (ec % 2)
        C.op("v", "tensor_copy", C.B_acc[:, 1024 * ec:1024 * ec + 512],
             C.PA[:, half:half + 512])
        C.op("v", "tensor_copy", C.B_acc[:, 1024 * ec + 512:1024 * (ec + 1)],
             C.PA[:, half + 512:half + 1024], inc=C.sv)
        C.mark[("wo_ev", ec)] = C.sv.v


def _p4_router(C):
    C.barrier("tv")
    r = C.Rsc
    for tc in range(NCH):
        if tc >= 1:
            C.wait("t", C.sv, C.mark[("rt1", tc - 1)])
            C.wait("t", C.sa, C.mark[("rt2", tc - 1)])
        for dc in range(DC):
            C.op("t", "matmul", C.PA[:, 0:8],
                 C.B_xn[:, 1024 * dc + 128 * tc:1024 * dc + 128 * (tc + 1)],
                 C.C_gwt[:, 8 * dc:8 * (dc + 1)],
                 start=(dc == 0), stop=(dc == DC - 1),
                 inc=(C.st if dc == DC - 1 else None))
        C.mark[("lg_mm", tc)] = C.st.v
        C.wait("v", C.st, C.mark[("lg_mm", tc)])
        C.op("v", "reduce_max", r[:, 32:33], C.PA[:, 0:8], axis=AX)
        C.op("v", "tensor_scalar", r[:, 33:34], r[:, 32:33], -1.0, None, ALU.mult,
             inc=C.sv)
        C.mark[("rt1", tc)] = C.sv.v
        C.wait("a", C.sv, C.mark[("rt1", tc)])
        C.op("a", "activation", r[:, 0:8], C.PA[:, 0:8], AF.Exp, bias=r[:, 33:34],
             inc=C.sa)
        C.mark[("rt2", tc)] = C.sa.v
        C.wait("v", C.sa, C.mark[("rt2", tc)])
        C.op("v", "reduce_sum", r[:, 32:33], r[:, 0:8], axis=AX)
        C.op("v", "reciprocal", r[:, 33:34], r[:, 32:33])
        C.op("v", "tensor_scalar", r[:, 0:8], r[:, 0:8], r[:, 33:34], None, ALU.mult)
        C.op("v", "tensor_tensor", r[:, 24:28], r[:, 0:8:2], r[:, 1:8:2], ALU.max)
        C.op("v", "reduce_max", r[:, 32:33], r[:, 24:28], axis=AX)
        C.op("v", "tensor_scalar", r[:, 28:32], r[:, 24:28], r[:, 32:33], None, ALU.is_ge)
        C.op("v", "tensor_scalar", r[:, 28:32], r[:, 28:32], BIG, None, ALU.mult)
        C.op("v", "tensor_sub", r[:, 28:32], r[:, 24:28], r[:, 28:32])
        C.op("v", "reduce_max", r[:, 33:34], r[:, 28:32], axis=AX)
        C.op("v", "tensor_scalar", r[:, 24:28], r[:, 24:28], r[:, 33:34], None, ALU.is_ge)
        C.op("v", "tensor_copy", r[:, 8:16:2], r[:, 24:28])
        C.op("v", "tensor_copy", r[:, 9:16:2], r[:, 24:28])
        C.op("v", "tensor_mul", r[:, 8:16], r[:, 0:8], r[:, 8:16])
        C.op("v", "reduce_max", r[:, 32:33], r[:, 8:16], axis=AX)
        C.op("v", "tensor_scalar", r[:, 16:24], r[:, 8:16], r[:, 32:33], None, ALU.is_ge)
        C.op("v", "tensor_scalar", r[:, 16:24], r[:, 16:24], BIG, None, ALU.mult)
        C.op("v", "tensor_sub", r[:, 16:24], r[:, 8:16], r[:, 16:24])
        C.op("v", "reduce_max", r[:, 33:34], r[:, 16:24], axis=AX)
        C.op("v", "tensor_scalar", r[:, 16:24], r[:, 8:16], r[:, 33:34], None, ALU.is_ge)
        C.op("v", "tensor_mul", C.Rwm[:, 8 * tc:8 * (tc + 1)], r[:, 8:16], r[:, 16:24],
             inc=C.sv)
        C.mark[("wm", tc)] = C.sv.v
    for tc in range(NCH):
        C.wait("t", C.sv, C.mark[("wm", tc)])
        if tc >= 1:
            C.wait("t", C.sv, C.mark[("wms", tc - 1)])
        C.op("t", "transpose", C.PB[0:8, 0:128], C.Rwm[:, 8 * tc:8 * (tc + 1)],
             C.C_identf[:, :], inc=C.st)
        C.mark[("wmt", tc)] = C.st.v
        C.wait("v", C.st, C.mark[("wmt", tc)])
        C.op("v", "tensor_copy", C.WMS[0:8, 128 * tc:128 * (tc + 1)], C.PB[0:8, 0:128],
             inc=C.sv)
        C.mark[("wms", tc)] = C.sv.v
    C.wait("y", C.sv, C.mark[("wms", NCH - 1)])
    C.op("y", "dma_start", out=C.dwms[0:8, :], in_=C.WMS[0:8, :], inc=C.sy, incv=16)
    C.mark["wmsd"] = C.sy.v


def _expert_ffn(C, n):
    C.wait("y", C.sy, C.mark["wmsd"])
    if ("wb_mm",) in C.mark:
        C.wait("y", C.st, C.mark[("wb_mm",)])
    C.op("y", "dma_start", out=C.WMR[0:1, :], in_=C.dwms[n, :][None, :], inc=C.sy, incv=16)
    C.mark["wmr"] = C.sy.v
    C.wait("t", C.sy, C.mark["wmr"])
    C.wait("t", C.sv)
    for tt in range(2):
        C.op("t", "matmul", C.PB[:, 512 * tt:512 * (tt + 1)], C.C_onesrow[0:1, :],
             C.WMR[0:1, 512 * tt:512 * (tt + 1)], start=True, stop=True,
             inc=(C.st if tt == 1 else None))
    C.mark[("wb_mm",)] = C.st.v
    C.wait("v", C.st, C.mark[("wb_mm",)])
    C.op("v", "tensor_copy", C.WB[:, :], C.PB[:, 0:1024], inc=C.sv)
    C.mark["wb"] = C.sv.v

    NSG = len(C.GUS)
    NSD = len(C.DS)
    for fc in range(FC):
        if fc >= NSG:
            C.wait("g", C.st, C.mark[("gu_mm", fc - NSG)])
        elif ("gu_mm", FC - 1) in C.mark:
            C.wait("g", C.st, C.mark[("gu_mm", FC - 1)])
        C.gslot(fc % NSG, C.GUS[fc % NSG][:, :], C.din["guw"][n, fc], ("gu_ld", fc))
        C.gwait("t", ("gu_ld", fc))
        if fc >= 1:
            C.wait("t", C.sv, C.mark[("uss_ev", fc - 1)])
            C.wait("t", C.sa, C.mark[("sg_ev", fc - 1)])
        elif ("uss_ev", FC - 1) in C.mark:
            C.wait("t", C.sv, C.mark[("uss_ev", FC - 1)])
            C.wait("t", C.sa, C.mark[("sg_ev", FC - 1)])
        for dc in range(DC):
            for tt in range(2):
                C.op("t", "matmul", C.PA[:, 512 * tt:512 * (tt + 1)],
                     C.GUS[fc % NSG][:, 128 * dc:128 * (dc + 1)],
                     C.B_xn[:, 1024 * dc + 512 * tt:1024 * dc + 512 * (tt + 1)],
                     start=(dc == 0), stop=(dc == DC - 1))
        for dc in range(DC):
            for tt in range(2):
                C.op("t", "matmul", C.PA[:, 1024 + 512 * tt:1024 + 512 * (tt + 1)],
                     C.GUS[fc % NSG][:, 2048 + 128 * dc:2048 + 128 * (dc + 1)],
                     C.B_xn[:, 1024 * dc + 512 * tt:1024 * dc + 512 * (tt + 1)],
                     start=(dc == 0), stop=(dc == DC - 1),
                     inc=(C.st if (tt == 1 and dc == DC - 1) else None))
        C.mark[("gu_mm", fc)] = C.st.v

        C.wait("a", C.st, C.mark[("gu_mm", fc)])
        if ("a_ev", fc - 2) in C.mark:
            C.wait("a", C.sv, C.mark[("a_ev", fc - 2)])
        C.op("a", "activation", C.SGS[fc % 2][:, :], C.PA[:, 0:1024], AF.Silu, inc=C.sa)
        C.mark[("sg_ev", fc)] = C.sa.v
        C.wait("v", C.st, C.mark[("gu_mm", fc)])
        C.wait("v", C.sv, C.mark["wb"])
        C.op("v", "tensor_mul", C.USS[fc % 2][:, :], C.PA[:, 1024:2048], C.WB[:, :],
             inc=C.sv)
        C.mark[("uss_ev", fc)] = C.sv.v
        C.wait("v", C.sa, C.mark[("sg_ev", fc)])
        C.op("v", "tensor_mul", C.A_E[fc][:, :], C.SGS[fc % 2][:, :], C.USS[fc % 2][:, :],
             inc=C.sv)
        C.mark[("a_ev", fc)] = C.sv.v

    for ec in range(DC):
        if ec >= NSD:
            C.wait("g", C.st, C.mark[("dn_mm", ec - NSD)])
        elif ("dn_mm", DC - 1) in C.mark:
            C.wait("g", C.st, C.mark[("dn_mm", DC - 1)])
        C.gslot(4 + ec % NSD, C.DS[ec % NSD][:, :], C.din["dw"][n, ec], ("dn_ld", ec))
        C.gwait("t", ("dn_ld", ec))
        C.wait("t", C.sv, C.mark[("a_ev", FC - 1)])
        C.wait("t", C.sa, C.mark[("sg_ev", FC - 1)])
        if ec >= 2:
            C.wait("t", C.sv, C.mark[("dn_ev", ec - 2)])
        half = 1024 * (ec % 2)
        for fc in range(FC):
            for tt in range(2):
                C.op("t", "matmul", C.PA[:, half + 512 * tt:half + 512 * (tt + 1)],
                     C.DS[ec % NSD][:, 128 * fc:128 * (fc + 1)],
                     C.A_E[fc][:, 512 * tt:512 * (tt + 1)],
                     start=(fc == 0), stop=(fc == FC - 1),
                     inc=(C.st if (tt == 1 and fc == FC - 1) else None))
        C.mark[("dn_mm", ec)] = C.st.v
        C.wait("v", C.st, C.mark[("dn_mm", ec)])
        for tt in range(2):
            C.op("v", "tensor_add",
                 C.B_acc[:, 1024 * ec + 512 * tt:1024 * ec + 512 * (tt + 1)],
                 C.B_acc[:, 1024 * ec + 512 * tt:1024 * ec + 512 * (tt + 1)],
                 C.PA[:, half + 512 * tt:half + 512 * (tt + 1)],
                 inc=(C.sv if tt == 1 else None))
        C.mark[("dn_ev", ec)] = C.sv.v


def _shared_ffn(C, half, fold_store=False):
    NSG = len(C.GUS)
    NSD = len(C.DS)
    for fc in range(FC):
        if fc >= NSG:
            C.wait("g", C.st, C.mark[("gu_mm", fc - NSG)])
        else:
            C.wait("g", C.st, C.mark[("gu_mm", FC - 1)])
        C.gslot(fc % NSG, C.GUS[fc % NSG][:, :], C.din["sguw"][11 * half + fc],
                ("gu_ld", fc))
        C.gwait("t", ("gu_ld", fc))
        if fc >= 1:
            C.wait("t", C.sv, C.mark[("uss_ev", fc - 1)])
            C.wait("t", C.sa, C.mark[("sg_ev", fc - 1)])
        else:
            C.wait("t", C.sv, C.mark[("uss_ev", FC - 1)])
            C.wait("t", C.sa, C.mark[("sg_ev", FC - 1)])
            C.wait("t", C.sv, C.mark[("dn_ev", DC - 1)])
        for dc in range(DC):
            for tt in range(2):
                C.op("t", "matmul", C.PA[:, 512 * tt:512 * (tt + 1)],
                     C.GUS[fc % NSG][:, 128 * dc:128 * (dc + 1)],
                     C.B_xn[:, 1024 * dc + 512 * tt:1024 * dc + 512 * (tt + 1)],
                     start=(dc == 0), stop=(dc == DC - 1))
        for dc in range(DC):
            for tt in range(2):
                C.op("t", "matmul", C.PA[:, 1024 + 512 * tt:1024 + 512 * (tt + 1)],
                     C.GUS[fc % NSG][:, 2048 + 128 * dc:2048 + 128 * (dc + 1)],
                     C.B_xn[:, 1024 * dc + 512 * tt:1024 * dc + 512 * (tt + 1)],
                     start=(dc == 0), stop=(dc == DC - 1),
                     inc=(C.st if (tt == 1 and dc == DC - 1) else None))
        C.mark[("gu_mm", fc)] = C.st.v

        C.wait("a", C.st, C.mark[("gu_mm", fc)])
        if ("a_ev", fc - 2) in C.mark:
            C.wait("a", C.sv, C.mark[("a_ev", fc - 2)])
        C.op("a", "activation", C.SGS[fc % 2][:, :], C.PA[:, 0:1024], AF.Silu, inc=C.sa)
        C.mark[("sg_ev", fc)] = C.sa.v
        C.wait("v", C.st, C.mark[("gu_mm", fc)])
        C.op("v", "tensor_copy", C.USS[fc % 2][:, :], C.PA[:, 1024:2048], inc=C.sv)
        C.mark[("uss_ev", fc)] = C.sv.v
        C.wait("v", C.sa, C.mark[("sg_ev", fc)])
        C.op("v", "tensor_mul", C.A_E[fc][:, :], C.SGS[fc % 2][:, :], C.USS[fc % 2][:, :],
             inc=C.sv)
        C.mark[("a_ev", fc)] = C.sv.v

    for ec in range(DC):
        if ec >= NSD:
            C.wait("g", C.st, C.mark[("dn_mm", ec - NSD)])
        else:
            C.wait("g", C.st, C.mark[("dn_mm", DC - 1)])
        C.gslot(4 + ec % NSD, C.DS[ec % NSD][:, :], C.din["sdw"][half, ec], ("dn_ld", ec))
        C.gwait("t", ("dn_ld", ec))
        C.wait("t", C.sv, C.mark[("a_ev", FC - 1)])
        C.wait("t", C.sa, C.mark[("sg_ev", FC - 1)])
        if ec >= 2:
            C.wait("t", C.sv, C.mark[("dn_ev", ec - 2)])
        hbank = 1024 * (ec % 2)
        for fc in range(FC):
            for tt in range(2):
                C.op("t", "matmul", C.PA[:, hbank + 512 * tt:hbank + 512 * (tt + 1)],
                     C.DS[ec % NSD][:, 128 * fc:128 * (fc + 1)],
                     C.A_E[fc][:, 512 * tt:512 * (tt + 1)],
                     start=(fc == 0), stop=(fc == FC - 1),
                     inc=(C.st if (tt == 1 and fc == FC - 1) else None))
        C.mark[("dn_mm", ec)] = C.st.v
        C.wait("v", C.st, C.mark[("dn_mm", ec)])
        for tt in range(2):
            C.op("v", "tensor_add",
                 C.B_acc[:, 1024 * ec + 512 * tt:1024 * ec + 512 * (tt + 1)],
                 C.B_acc[:, 1024 * ec + 512 * tt:1024 * ec + 512 * (tt + 1)],
                 C.PA[:, hbank + 512 * tt:hbank + 512 * (tt + 1)],
                 inc=(C.sv if tt == 1 else None))
        C.mark[("dn_ev", ec)] = C.sv.v
        if fold_store:
            if ec >= 2:
                C.wait("y", C.sv, C.mark[("p7add", ec - 2)])
            C.op("y", "dma_start", out=C.XST[ec % 2][:, :], in_=C.din["x"][ec],
                 inc=C.sy, incv=16)
            C.mark[("p7ld", ec)] = C.sy.v
            C.wait("v", C.sy, C.mark[("p7ld", ec)])
            C.op("v", "tensor_add", C.B_acc[:, 1024 * ec:1024 * (ec + 1)],
                 C.B_acc[:, 1024 * ec:1024 * (ec + 1)], C.XST[ec % 2][:, :], inc=C.sv)
            C.mark[("p7add", ec)] = C.sv.v
            C.wait("y", C.sv, C.mark[("p7add", ec)])
            C.op("y", "dma_start", out=C.dout[ec],
                 in_=C.B_acc[:, 1024 * ec:1024 * (ec + 1)], inc=C.sy, incv=16)


def _p7_store(C):
    C.wait("y", C.sy)


def _emit(C):
    C.reset()
    _consts_phase(C)
    _p0_layernorm(C)
    _p1_kv(C)
    _a2_m_collective(C)
    _p3_q_g1(C)
    _p3b_consts2_sstart(C)
    _a3_attention(C)
    _a4_wo(C)
    _p4_router(C)
    C.barrier("tvag")
    for n in range(NE):
        _expert_ffn(C, n)
    _shared_ffn(C, 0)
    _shared_ffn(C, 1, fold_store=True)
    _p7_store(C)


def build_nc():
    nc = bass.Bass("TRN2", target_bir_lowering=False)
    C = K()
    C.nc = nc

    din = {}
    din["x"] = nc.dram_tensor("x", [DC, 128, T], F32, kind="ExternalInput")
    for nm, shp in (
        ("wq", [DC, 128, 2048]), ("wk", [DC, 128, 2048]), ("wv", [DC, 128, 2048]),
        ("wo", [DC, 128, 2048]), ("wg1", [128, DC, 128]), ("wg2", [128, 2048]),
        ("gwt", [128, DC, 8]), ("nw", [128, H]),
        ("ldr", [1, H]), ("ldc", [H, 1]),
        ("guw", [NE, FC, 128, 4096]),
        ("dw", [NE, DC, 128, 1408]),
        ("sguw", [FSC, 128, 4096]),
        ("sdw", [2, DC, 128, 1408]),
        ("identf", [128, 128]), ("difft", [128, 128]), ("maskt", [128, 128]),
        ("onesrow", [1, 128]), ("onescol", [128, 1]), ("cposrow", [1, 128]),
        ("revtrow", [1, 128]), ("cdlrow", [1, 128]), ("sel16", [H, 128]),
        ("scmm", [128, 3]), ("scmk", [128, 3]),
    ):
        din[nm] = nc.dram_tensor(nm, shp, F32, kind="ExternalInput")
    C.din = din
    C.dout = nc.dram_tensor("out", [DC, 128, T], F32, kind="ExternalOutput")
    C.dcin = nc.dram_tensor("cin", [H * 128, 128], F32, kind="Internal")
    C.dcout = nc.dram_tensor("cout", [4 * H * 128, 128], F32, kind="Internal")
    C.dwms = nc.dram_tensor("wmsd", [NE, 1024], F32, kind="Internal")

    A = nc.alloc_sbuf_tensor
    C.B_acc = A("B_acc", [128, 16384], F32)
    C.B_xn = A("B_xn", [128, 16384], BF16)
    C.B_q = A("B_q", [128, 16384], BF16)
    C.B_k = A("B_k", [128, 16384], BF16)
    C.B_v = A("B_v", [128, 16384], BF16)
    C.C_cw = A("C_cw", [128, 896], F32)
    C.C_identf = A("C_identf", [128, 128], F32)
    C.C_identb = A("C_identb", [128, 128], BF16)
    C.C_difft = A("C_difft", [128, 128], F32)
    C.C_maskt = A("C_maskt", [128, 128], F32)
    C.C_wg2 = A("C_wg2", [128, 2048], BF16)
    C.C_gwt = A("C_gwt", [128, DC * 8], BF16)
    C.C_nw = A("C_nw", [128, H], F32)
    C.C_ldr = A("C_ldr", [1, H], F32)
    C.C_ldc = A("C_ldc", [H, 1], F32)
    C.C_sel16 = A("C_sel16", [H, 128], F32)
    C.C_onesrow = A("C_onesrow", [1, 128], F32)
    C.C_onescol = A("C_onescol", [128, 1], F32)
    C.C_cposrow = A("C_cposrow", [1, 128], F32)
    C.C_revtrow = A("C_revtrow", [1, 128], F32)
    C.C_cdlrow = A("C_cdlrow", [1, 128], F32)
    C.C_scmm = A("C_scmm", [128, 3], F32)
    C.C_scmk = A("C_scmk", [128, 3], F32)
    C.Rwm = A("Rwm", [128, 64], F32)
    C.Rsc = A("Rsc", [128, 64], F32)

    AT = nc.alloc_sbuf_tensor_at
    ab = nc.lookup_mloc(C.B_acc).addr
    qb = nc.lookup_mloc(C.B_q).addr
    kb = nc.lookup_mloc(C.B_k).addr
    vb = nc.lookup_mloc(C.B_v).addr
    C.VSall = [AT(f"VSall{i}", [128, 2048], BF16, offset=ab + 4096 * i) for i in range(NCH)]
    C.KTMall = [AT(f"KTMall{i}", [128, 2048], BF16, offset=ab + 32768 + 4096 * i)
                for i in range(NCH)]
    C.WSLAB = [AT(f"WSLAB{i}", [128, 2048], BF16, offset=ab + 4096 * i) for i in range(4)]
    C.VST = [AT(f"VST{i}", [128, 1024], BF16, offset=ab + 16384 + 2048 * i) for i in range(2)]
    C.Mst = AT("Mst", [128, 2048], F32, offset=ab + 0)
    C.CDB = AT("CDB", [128, 2048], F32, offset=ab + 0)
    C.G1W = AT("G1W", [128, 2048], BF16, offset=ab + 8192)
    C.G1 = AT("G1", [128, 1024], BF16, offset=ab + 8192)
    C.QS = AT("QS", [128, 2048], BF16, offset=ab + 12288)
    C.S = AT("S", [128, 2048], F32, offset=ab + 16384)
    C.S_bf = AT("S_bf", [128, 2048], BF16, offset=ab + 24576)
    C.DTm = AT("DTm", [128, 2048], F32, offset=ab + 28672)
    C.QDB = AT("QDB", [128, 2048], BF16, offset=ab + 36864)
    C.KTM = AT("KTM", [128, 2048], BF16, offset=ab + 40960)
    C.STm = AT("STm", [128, 2048], BF16, offset=ab + 45056)
    C.SIG = AT("SIG", [128, 2048], BF16, offset=ab + 49152)
    C.OG = AT("OG", [128, 2048], F32, offset=ab + 53248)
    C.OTM = AT("OTM", [128, 2048], BF16, offset=ab + 61440)
    C.Xsq = [AT(f"Xsq{i}", [128, 1024], F32, offset=qb + 4096 * i) for i in range(2)]
    C.Emean = AT("Emean", [1, 1024], F32, offset=qb + 16384)
    C.Evar = AT("Evar", [1, 1024], F32, offset=qb + 20480)
    C.Emrs = AT("Emrs", [1, 1024], F32, offset=qb + 24576)
    C.Brstd = AT("Brstd", [128, 1024], F32, offset=qb + 0)
    C.Bmrs = AT("Bmrs", [128, 1024], F32, offset=qb + 8192)
    C.A_E = [AT(f"A_E{i}", [128, 1024], BF16, offset=qb + 2048 * i) for i in range(FC)]
    C.GUS = [AT(f"GUS{i}", [128, 4096], BF16, offset=kb + 8192 * i) for i in range(3)]
    C.SGS = [AT(f"SGS{i}", [128, 1024], BF16, offset=kb + 24576 + 2048 * i) for i in range(2)]
    C.USS = [AT(f"USS{i}", [128, 1024], BF16, offset=kb + 28672 + 2048 * i) for i in range(2)]
    C.DS = [AT(f"DS{i}", [128, 1408], BF16, offset=qb + 22528 + 2816 * i) for i in range(3)]
    C.WOSLAB = [AT(f"WOSLAB{i}", [128, 2048], BF16, offset=vb + 4096 * i) for i in range(2)]
    C.WMS = AT("WMS", [8, 1024], F32, offset=vb + 8704)
    C.WB = AT("WB", [128, 1024], F32, offset=vb + 12800)
    C.XST = [AT(f"XST{i}", [128, 1024], F32, offset=vb + 16896 + 4096 * i) for i in range(2)]
    C.WMR = AT("WMR", [1, 1024], F32, offset=vb + 28672)

    C.PA = nc.alloc_psum_tensor("PA", [128, 2048], F32)
    C.PB = nc.alloc_psum_tensor("PB", [128, 1024], F32)
    C.PC = nc.alloc_psum_tensor("PC", [128, 2048], BF16)

    C.sgp = [_Sem(nc.alloc_semaphore(name=f"sgp{i}")) for i in range(7)]
    C.sgx = _Sem(nc.alloc_semaphore(name="sgx"))
    C.st = _Sem(nc.alloc_semaphore(name="st"))
    C.sv = _Sem(nc.alloc_semaphore(name="sv"))
    C.sa = _Sem(nc.alloc_semaphore(name="sa"))
    C.sy = _Sem(nc.alloc_semaphore(name="sy"))
    C.sg = _Sem(nc.alloc_semaphore(name="sg"))
    C.scc = _Sem(nc.alloc_semaphore(name="scc"))

    with nc.Block() as block:
        @block.tensor
        def _(e):
            C.cur, C.h = "t", e
            _emit(C)

        @block.vector
        def _(e):
            C.cur, C.h = "v", e
            _emit(C)

        @block.scalar
        def _(e):
            C.cur, C.h = "a", e
            _emit(C)

        @block.gpsimd
        def _(e):
            C.cur, C.h = "g", e
            _emit(C)

        @block.sync
        def _(e):
            C.cur, C.h = "y", e
            _emit(C)

    _dedupe_ldweights(nc)
    return nc


def _dedupe_ldweights(nc):
    n = 0
    for func in nc.m.functions:
        for blk in func.blocks:
            insts = blk.instructions
            prev_key = None
            for idx, inst in enumerate(insts):
                if isinstance(inst, mybir.InstMatmult) and not inst.is_transpose:
                    key = str(inst.ins[1])
                    if key == prev_key and "float32" not in str(inst.ins[1].dtype):
                        insts[idx] = dataclasses.replace(inst, ldweights=False)
                        n += 1
                    prev_key = key
                else:
                    prev_key = None
    return n



def _lhsT_slabs(W, kc, mc):
    A = W.reshape(kc, 128, mc, 128).transpose(2, 1, 0, 3)
    return np.ascontiguousarray(A).reshape(mc, 128, kc * 128)


def _prep_shared(inputs):
    f = np.float32
    d = {}
    d["wq"] = _lhsT_slabs(np.asarray(inputs["L0_attn_Wq"], f), DC, DC)
    d["wk"] = _lhsT_slabs(np.asarray(inputs["L0_attn_Wk"], f), DC, DC)
    d["wv"] = _lhsT_slabs(np.asarray(inputs["L0_attn_Wv"], f), DC, DC)
    d["wo"] = _lhsT_slabs(np.asarray(inputs["L0_attn_Wo"], f), DC, DC)
    d["wg1"] = np.ascontiguousarray(
        np.asarray(inputs["L0_attn_Wg1"], f).reshape(DC, 128, 128).transpose(1, 0, 2))
    d["wg2"] = np.ascontiguousarray(np.asarray(inputs["L0_attn_Wg2"], f))
    d["gwt"] = np.ascontiguousarray(
        np.asarray(inputs["L0_ffn_gate_weight"], f).T.reshape(DC, 128, NE).transpose(1, 0, 2))
    d["nw"] = np.ascontiguousarray(np.asarray(inputs["L0_attn_norm_w"], f).reshape(H, 128).T)
    ld = np.asarray(inputs["L0_attn_log_decay"], f)
    d["ldr"] = ld.reshape(1, H).copy()
    d["ldc"] = ld.reshape(H, 1).copy()
    gw = np.asarray(inputs["L0_ffn_gate_w"], f)
    uw = np.asarray(inputs["L0_ffn_up_w"], f)
    dw = np.asarray(inputs["L0_ffn_down_w"], f)
    gws = np.stack([_lhsT_slabs(gw[n].T, DC, FC) for n in range(NE)])
    uws = np.stack([_lhsT_slabs(uw[n].T, DC, FC) for n in range(NE)])
    d["guw"] = np.ascontiguousarray(np.concatenate([gws, uws], axis=-1))
    d["dw"] = np.stack([_lhsT_slabs(dw[n].T, FC, DC) for n in range(NE)])
    sgws = _lhsT_slabs(np.asarray(inputs["L0_ffn_sh_gate_w"], f).T, DC, FSC)
    suws = _lhsT_slabs(np.asarray(inputs["L0_ffn_sh_up_w"], f).T, DC, FSC)
    d["sguw"] = np.ascontiguousarray(np.concatenate([sgws, suws], axis=-1))
    sd = np.asarray(inputs["L0_ffn_sh_down_w"], f).T.reshape(FSC, 128, DC, 128)
    sdf = sd.transpose(2, 1, 0, 3)
    d["sdw"] = np.ascontiguousarray(
        np.stack([sdf[:, :, 0:FC, :], sdf[:, :, FC:FSC, :]])).reshape(2, DC, 128, FC * 128)
    idx = np.arange(128, dtype=f)
    d["identf"] = np.eye(128, dtype=f)
    d["difft"] = np.maximum(idx[None, :] - idx[:, None], 0.0).astype(f)
    d["maskt"] = (idx[None, :] >= idx[:, None]).astype(f)
    d["onesrow"] = np.ones((1, 128), f)
    d["onescol"] = np.ones((128, 1), f)
    d["cposrow"] = (idx + 1.0).reshape(1, 128).astype(f)
    d["revtrow"] = (127.0 - idx).reshape(1, 128).astype(f)
    d["cdlrow"] = np.tile(128.0 * (7.0 - np.arange(8, dtype=f)), H).reshape(1, 128)
    sel = np.zeros((H, 128), f)
    for h in range(H):
        sel[h, 8 * h:8 * (h + 1)] = 1.0
    d["sel16"] = sel
    return d


def kernel(**inputs):
    shared = _prep_shared(inputs)
    x = np.asarray(inputs["L0_attn_x"], np.float32).reshape(NCORE * T, D)
    nc = build_nc()
    in_maps = []
    for c in range(NCORE):
        m = dict(shared)
        shard = x[T * c:T * (c + 1)].reshape(T, DC, 128).transpose(1, 2, 0)
        m["x"] = np.ascontiguousarray(shard)
        j = c % 4
        mult = np.array([j - 1 - jp for jp in range(3)], np.float32)
        msk = np.array([1.0 if jp < j else 0.0 for jp in range(3)], np.float32)
        m["scmm"] = np.tile(1024.0 * mult * msk, (128, 1)).astype(np.float32)
        m["scmk"] = np.tile(msk, (128, 1)).astype(np.float32)
        in_maps.append(m)
    res = run_bass_kernel_spmd(nc, in_maps, core_ids=list(range(NCORE)))
    shards = []
    for c in range(NCORE):
        o = res.results[c]["out"]
        shards.append(np.ascontiguousarray(np.transpose(o, (2, 0, 1))).reshape(T, D))
    return np.concatenate(shards, axis=0).reshape(2, 4096, D).astype(np.float32)


if __name__ == "__main__":
    nc = build_nc()
    print("instructions:", nc.n_instructions())
    print("sbuf remaining:", nc.sbuf_bytes_remaining)


# version 10 (final)
# speedup vs baseline: 1.0351x; 1.0146x over previous
import dataclasses

import numpy as np

import concourse.bass as bass
import concourse.mybir as mybir
from concourse.bass_utils import run_bass_kernel_spmd

F32 = mybir.dt.float32
BF16 = mybir.dt.bfloat16
AF = mybir.ActivationFunctionType
ALU = mybir.AluOpType
AX = mybir.AxisListType.X

NCORE = 8
D = 2048
T = 1024
DC = 16
NCH = 8
H = 16
NE = 8
FC = 11
FSC = 22
EPS = 1e-5
BIG = 1e30


class _Sem:
    __slots__ = ("h", "v")

    def __init__(self, h):
        self.h = h
        self.v = 0


class K:

    def reset(self):
        for s in (self.st, self.sv, self.sa, self.sy, self.sg, self.scc,
                  self.sgx, *self.sgp):
            s.v = 0
        self.mark = {}

    def op(self, ek, fname, *args, inc=None, incv=1, **kw):
        if ek in ("v", "a"):
            s = self.sv if ek == "v" else self.sa
            assert inc is None or inc is s
            pre = s.v
            s.v += 1
            if self.cur == ek:
                self.h.wait_ge(s.h, pre)
                getattr(self.h, fname)(*args, **kw).then_inc(s.h, 1)
            return
        if inc is not None:
            inc.v += incv
        if self.cur == ek:
            inst = getattr(self.h, fname)(*args, **kw)
            if inc is not None:
                inst.then_inc(inc.h, incv)

    def wait(self, ek, sem, v=None):
        if self.cur == ek:
            self.h.wait_ge(sem.h, sem.v if v is None else v)

    def gslot(self, slot, out, in_, key):
        self.op("g", "dma_start", out=out, in_=in_, inc=self.sgp[slot], incv=16)
        self.mark[key] = (slot, self.sgp[slot].v)

    def gwait(self, ek, key):
        slot, v = self.mark[key]
        self.wait(ek, self.sgp[slot], v)

    def barrier(self, engines="tva"):
        for ek in engines:
            for s in (self.st, self.sv, self.sa, self.sy, self.sg,
                      self.sgx, *self.sgp):
                if s.v > 0:
                    self.wait(ek, s)



def _consts_phase(C):
    cw = C.C_cw
    for name, dst in (
        ("identf", C.C_identf), ("difft", C.C_difft), ("maskt", C.C_maskt),
        ("ldr", C.C_ldr), ("sel16", C.C_sel16), ("ldc", C.C_ldc),
        ("onesrow", C.C_onesrow), ("onescol", C.C_onescol),
        ("cposrow", C.C_cposrow), ("revtrow", C.C_revtrow),
        ("cdlrow", C.C_cdlrow), ("scmm", C.C_scmm), ("scmk", C.C_scmk),
        ("nw", C.C_nw),
    ):
        sl = tuple(slice(0, s) for s in dst.shape)
        C.op("y", "dma_start", out=dst[sl], in_=C.din[name][sl], inc=C.sy, incv=16)
    C.op("g", "dma_start", out=C.C_identb[:, :], in_=C.din["identf"][:, :], inc=C.sg, incv=16)
    C.op("g", "dma_start", out=C.C_wg2[:, :], in_=C.din["wg2"][:, :], inc=C.sg, incv=16)
    C.op("g", "dma_start", out=C.C_gwt[:, :],
         in_=C.din["gwt"].ap().rearrange("p a b -> p (a b)"), inc=C.sg, incv=16)

    C.wait("t", C.sy)
    C.op("t", "matmul", C.PA[:, 0:16], C.C_onesrow[0:1, :], C.C_ldr[0:1, :],
         start=True, stop=True)
    C.op("t", "matmul", C.PA[:, 16:32], C.C_revtrow[0:1, :], C.C_ldr[0:1, :],
         start=True, stop=True)
    C.op("t", "matmul", C.PA[0:1, 48:176], C.C_ldc[0:H, 0:1], C.C_sel16[0:H, :],
         start=True, stop=True, inc=C.st)
    C.wait("a", C.st)
    C.op("a", "activation", cw[:, 0:16], C.PA[:, 0:16], AF.Copy)
    C.op("a", "activation", cw[:, 16:32], C.PA[:, 16:32], AF.Exp)
    C.op("a", "activation", cw[0:1, 192:320], C.PA[0:1, 48:176], AF.Copy, inc=C.sa)
    C.wait("v", C.sa)
    C.op("v", "tensor_scalar", cw[0:1, 320:336], C.C_ldr[0:1, :], 128.0, None, ALU.mult)
    C.op("v", "tensor_mul", cw[0:1, 336:464], cw[0:1, 192:320], C.C_cdlrow[0:1, :], inc=C.sv)
    C.wait("a", C.sv)
    C.op("a", "activation", cw[0:1, 464:480], cw[0:1, 320:336], AF.Exp)
    C.op("a", "activation", cw[0:1, 480:608], cw[0:1, 336:464], AF.Exp, inc=C.sa)
    C.wait("t", C.sa)
    C.op("t", "matmul", C.PA[:, 48:176], C.C_onesrow[0:1, :], cw[0:1, 480:608],
         start=True, stop=True)
    C.op("t", "matmul", C.PA[:, 176:192], C.C_onesrow[0:1, :], cw[0:1, 464:480],
         start=True, stop=True, inc=C.st)
    C.wait("v", C.st)
    C.op("v", "tensor_copy", cw[:, 64:192], C.PA[:, 48:176])
    C.op("v", "tensor_copy", cw[:, 48:64], C.PA[:, 176:192], inc=C.sv)


def _p0_layernorm(C):
    for dc in range(DC):
        C.op("y", "dma_start", out=C.B_acc[:, 1024 * dc:1024 * (dc + 1)],
             in_=C.din["x"][dc], inc=C.sy, incv=16)
        C.mark[("xld", dc)] = C.sy.v
    for dc in range(DC):
        C.wait("a", C.sy, C.mark[("xld", dc)])
        if dc >= 2:
            C.wait("a", C.st, C.mark[("p0mm", dc - 2)])
        C.op("a", "activation", C.Xsq[dc % 2][:, :], C.B_acc[:, 1024 * dc:1024 * (dc + 1)],
             AF.Square, inc=C.sa)
        C.mark[("p0sq", dc)] = C.sa.v
        C.wait("t", C.sa, C.mark[("p0sq", dc)])
        C.wait("t", C.sy, C.mark[("xld", dc)])
        for tt in range(2):
            C.op("t", "matmul", C.PB[0:1, 512 * tt:512 * (tt + 1)],
                 C.C_onescol[:, 0:1],
                 C.B_acc[:, 1024 * dc + 512 * tt:1024 * dc + 512 * (tt + 1)],
                 start=(dc == 0), stop=(dc == DC - 1))
            C.op("t", "matmul", C.PA[0:1, 512 * tt:512 * (tt + 1)],
                 C.C_onescol[:, 0:1], C.Xsq[dc % 2][:, 512 * tt:512 * (tt + 1)],
                 start=(dc == 0), stop=(dc == DC - 1),
                 inc=(C.st if tt == 1 else None))
        C.mark[("p0mm", dc)] = C.st.v
    C.wait("v", C.st)
    C.op("v", "tensor_scalar", C.Emean[0:1, :], C.PB[0:1, 0:1024], 1.0 / D, None, ALU.mult)
    C.op("v", "tensor_scalar", C.Evar[0:1, :], C.PA[0:1, 0:1024], 1.0 / D, None, ALU.mult)
    C.op("v", "tensor_mul", C.Emrs[0:1, :], C.Emean[0:1, :], C.Emean[0:1, :])
    C.op("v", "tensor_sub", C.Evar[0:1, :], C.Evar[0:1, :], C.Emrs[0:1, :], inc=C.sv)
    C.op("v", "tensor_scalar", C.Evar[0:1, :], C.Evar[0:1, :], EPS, None, ALU.add, inc=C.sv)
    C.wait("a", C.sv)
    C.op("a", "activation", C.Evar[0:1, :], C.Evar[0:1, :], AF.Sqrt, inc=C.sa)
    C.wait("v", C.sa)
    C.op("v", "reciprocal", C.Evar[0:1, :], C.Evar[0:1, :])
    C.op("v", "tensor_mul", C.Emrs[0:1, :], C.Emean[0:1, :], C.Evar[0:1, :], inc=C.sv)
    C.wait("t", C.sv)
    for tt in range(2):
        C.op("t", "matmul", C.PA[:, 512 * tt:512 * (tt + 1)], C.C_onesrow[0:1, :],
             C.Evar[0:1, 512 * tt:512 * (tt + 1)], start=True, stop=True)
        C.op("t", "matmul", C.PA[:, 1024 + 512 * tt:1024 + 512 * (tt + 1)],
             C.C_onesrow[0:1, :], C.Emrs[0:1, 512 * tt:512 * (tt + 1)],
             start=True, stop=True, inc=(C.st if tt == 1 else None))
    C.wait("v", C.st)
    C.op("v", "tensor_copy", C.Brstd[:, :], C.PA[:, 0:1024])
    C.op("v", "tensor_copy", C.Bmrs[:, :], C.PA[:, 1024:2048], inc=C.sv)
    for dc in range(DC):
        C.op("v", "tensor_mul", C.Xsq[1][:, :], C.B_acc[:, 1024 * dc:1024 * (dc + 1)],
             C.Brstd[:, :])
        C.op("v", "tensor_sub", C.B_xn[:, 1024 * dc:1024 * (dc + 1)], C.Xsq[1][:, :],
             C.Bmrs[:, :], inc=(C.sv if dc == DC - 1 else None))
    C.mark["xn"] = C.sv.v


def _projection(C, wname, dst, pre_evac_wait=None, post=None):
    C.barrier("g")
    C.barrier("t")
    NS = len(C.WSLAB)
    for ec in range(DC):
        if ec >= NS:
            C.wait("g", C.st, C.mark[("proj_mm", wname, ec - NS)])
        C.gslot(ec % NS, C.WSLAB[ec % NS][:, :], C.din[wname][ec],
                ("proj_ld", wname, ec))
        C.gwait("t", ("proj_ld", wname, ec))
        if ec >= 2:
            C.wait("t", C.sa, C.mark[("proj_ev", wname, ec - 2)])
        for dc in range(DC):
            for tt in range(2):
                bank = 1024 * (ec % 2) + 512 * tt
                C.op("t", "matmul", C.PA[:, bank:bank + 512],
                     C.WSLAB[ec % NS][:, 128 * dc:128 * (dc + 1)],
                     C.B_xn[:, 1024 * dc + 512 * tt:1024 * dc + 512 * (tt + 1)],
                     start=(dc == 0), stop=(dc == DC - 1),
                     inc=(C.st if (tt == 1 and dc == DC - 1) else None))
        C.mark[("proj_mm", wname, ec)] = C.st.v

        C.wait("a", C.st, C.mark[("proj_mm", wname, ec)])
        if pre_evac_wait is not None:
            pre_evac_wait(ec)
        half = 1024 * (ec % 2)
        C.op("a", "activation", dst(ec)[:, 0:512], C.PA[:, half:half + 512], AF.Silu)
        C.op("a", "activation", dst(ec)[:, 512:1024], C.PA[:, half + 512:half + 1024],
             AF.Silu, inc=C.sa)
        C.mark[("proj_ev", wname, ec)] = C.sa.v
        if post is not None:
            post(ec)


def _p1_kv(C):
    _projection(C, "wk", lambda ec: C.B_k[:, 1024 * ec:1024 * (ec + 1)])

    def v_pre_wait(ec):
        if ec >= 2:
            C.wait("a", C.st, C.mark[("vtr", ec - 2)])

    def v_post(ec):
        C.wait("t", C.sa, C.mark[("proj_ev", "wv", ec)])
        if ec >= 2:
            C.wait("t", C.sv, C.mark[("vtr_ev", ec - 2)])
        half = 1024 * (ec % 2)
        for ch in range(NCH):
            C.op("t", "transpose", C.PC[:, half + 128 * ch:half + 128 * (ch + 1)],
                 C.VST[ec % 2][:, 128 * ch:128 * (ch + 1)], C.C_identb[:, :],
                 inc=(C.st if ch == NCH - 1 else None))
        C.mark[("vtr", ec)] = C.st.v
        C.wait("v", C.st, C.mark[("vtr", ec)])
        for ch in range(NCH):
            C.op("v", "tensor_copy",
                 C.B_v[:, 2048 * ch + 128 * ec:2048 * ch + 128 * (ec + 1)],
                 C.PC[:, half + 128 * ch:half + 128 * (ch + 1)],
                 inc=(C.sv if ch == NCH - 1 else None))
        C.mark[("vtr_ev", ec)] = C.sv.v

    _projection(C, "wv", lambda ec: C.VST[ec % 2], pre_evac_wait=v_pre_wait, post=v_post)
    C.barrier("tva")


def _a2_m_collective(C):
    cw = C.C_cw
    C.barrier("tva")
    C.op("v", "memset", C.STm[:, 0:128], 1.0)
    for h in range(H):
        C.op("v", "tensor_scalar", C.KDB[:, 128 * h:128 * (h + 1)], C.STm[:, 0:128],
             cw[:, 16 + h:17 + h], None, ALU.mult, inc=(C.sv if h == H - 1 else None))
    C.mark["kdb"] = C.sv.v
    for h in range(H):
        C.op("v", "tensor_scalar", cw[0:1, 336:464], C.C_onesrow[0:1, :],
             cw[0:1, 464 + h:465 + h], None, ALU.mult, inc=C.sv)
        C.mark[("c2r", h)] = C.sv.v
        C.wait("t", C.sv, C.mark[("c2r", h)])
        if h >= 4:
            C.wait("t", C.sv, C.mark[("c2e", h - 4)])
        C.op("t", "matmul", C.PA[:, 128 * (h % 4):128 * (h % 4 + 1)],
             C.C_onesrow[0:1, :], cw[0:1, 336:464], start=True, stop=True, inc=C.st)
        C.mark[("c2m", h)] = C.st.v
        C.wait("v", C.st, C.mark[("c2m", h)])
        C.op("v", "tensor_copy", C.CDB2[:, 128 * h:128 * (h + 1)],
             C.PA[:, 128 * (h % 4):128 * (h % 4 + 1)], inc=C.sv)
        C.mark[("c2e", h)] = C.sv.v
    for h in range(H):
        C.op("v", "tensor_scalar", cw[0:1, 336:464], C.C_cposrow[0:1, :],
             C.C_ldr[0:1, h:h + 1], None, ALU.mult, inc=C.sv)
        C.mark[("qr1", h)] = C.sv.v
        C.wait("a", C.sv, C.mark[("qr1", h)])
        C.op("a", "activation", cw[0:1, 464:592], cw[0:1, 336:464], AF.Exp, inc=C.sa)
        C.mark[("qr2", h)] = C.sa.v
        C.wait("t", C.sa, C.mark[("qr2", h)])
        if h >= 4:
            C.wait("t", C.sv, C.mark[("qdbev", h - 4)])
        C.op("t", "matmul", C.PA[:, 128 * (h % 4):128 * (h % 4 + 1)],
             C.C_onesrow[0:1, :], cw[0:1, 464:592], start=True, stop=True, inc=C.st)
        C.mark[("qdbmm", h)] = C.st.v
        C.wait("v", C.st, C.mark[("qdbmm", h)])
        C.op("v", "tensor_copy", C.QDB[:, 128 * h:128 * (h + 1)],
             C.PA[:, 128 * (h % 4):128 * (h % 4 + 1)], inc=C.sv)
        C.mark[("qdbev", h)] = C.sv.v
    for h in range(H):
        C.op("v", "tensor_scalar", C.STm[:, 128:256], C.C_difft[:, :], cw[:, h:h + 1],
             None, ALU.mult, inc=C.sv)
        C.mark[("dtm1", h)] = C.sv.v
        C.wait("a", C.sv, C.mark[("dtm1", h)])
        C.op("a", "activation", C.STm[:, 256:384], C.STm[:, 128:256], AF.Exp, inc=C.sa)
        C.mark[("dtm2", h)] = C.sa.v
        C.wait("v", C.sa, C.mark[("dtm2", h)])
        C.op("v", "tensor_mul", C.DTm[:, 128 * h:128 * (h + 1)], C.STm[:, 256:384],
             C.C_maskt[:, :], inc=C.sv)
    for jp in range(3):
        C.op("v", "tensor_scalar", cw[:, 832:848], cw[:, 0:16], C.C_scmm[:, jp:jp + 1],
             None, ALU.mult, inc=C.sv)
        C.mark[("cf1", jp)] = C.sv.v
        C.wait("a", C.sv, C.mark[("cf1", jp)])
        C.op("a", "activation", cw[:, 848:864], cw[:, 832:848], AF.Exp, inc=C.sa)
        C.mark[("cf2", jp)] = C.sa.v
        C.wait("v", C.sa, C.mark[("cf2", jp)])
        C.op("v", "tensor_scalar", cw[:, 736 + 16 * jp:752 + 16 * jp], cw[:, 848:864],
             C.C_scmk[:, jp:jp + 1], None, ALU.mult, inc=C.sv)
    C.mark["coef"] = C.sv.v
    for ch in range(NCH):
        if ch >= 1:
            C.wait("t", C.sv, C.mark[("a2ktm", ch - 1)])
        else:
            C.wait("t", C.sv, C.mark["coef"])
        for h in range(H):
            C.op("t", "transpose", C.PC[:, 128 * h:128 * (h + 1)],
                 C.B_k[:, 1024 * h + 128 * ch:1024 * h + 128 * (ch + 1)],
                 C.C_identb[:, :], inc=(C.st if h == H - 1 else None))
        C.mark[("a2tr", ch)] = C.st.v
        C.wait("v", C.st, C.mark[("a2tr", ch)])
        C.op("v", "tensor_mul", C.KTM[:, :], C.PC[:, :], C.KDB[:, :], inc=C.sv)
        C.mark[("a2ktm", ch)] = C.sv.v
        for w in range(2):
            C.wait("t", C.sv, C.mark[("a2ktm", ch)])
            if ch > 0 or w > 0:
                C.wait("t", C.sv, C.mark[("hor", ch, w - 1) if w > 0 else ("hor", ch - 1, 1)])
            for i in range(8):
                h = 8 * w + i
                C.op("t", "matmul", C.PB[:, 128 * i:128 * (i + 1)],
                     C.KTM[:, 128 * h:128 * (h + 1)],
                     C.B_v[:, 2048 * ch + 128 * h:2048 * ch + 128 * (h + 1)],
                     start=True, stop=True, inc=(C.st if i == 7 else None))
            C.mark[("a2kv", ch, w)] = C.st.v
            C.wait("v", C.st, C.mark[("a2kv", ch, w)])
            if ch == 0:
                C.op("v", "tensor_copy", C.Mst[:, 1024 * w:1024 * (w + 1)],
                     C.PB[:, 0:1024], inc=C.sv)
            else:
                C.op("v", "tensor_mul", C.Mst[:, 1024 * w:1024 * (w + 1)],
                     C.Mst[:, 1024 * w:1024 * (w + 1)], C.CDB2[:, 1024 * w:1024 * (w + 1)])
                C.op("v", "tensor_add", C.Mst[:, 1024 * w:1024 * (w + 1)],
                     C.Mst[:, 1024 * w:1024 * (w + 1)], C.PB[:, 0:1024], inc=C.sv)
            C.mark[("hor", ch, w)] = C.sv.v
    C.mark["m_done"] = C.sv.v
    C.wait("y", C.sv, C.mark["m_done"])
    C.op("y", "dma_start", out=C.dcin.ap().rearrange("(h p) e -> p h e", p=128),
         in_=C.Mst[:, :].rearrange("p (h e) -> p h e", h=H), inc=C.sy, incv=16)
    C.mark["cin"] = C.sy.v
    C.wait("g", C.sy, C.mark["cin"])
    C.op("g", "collective_compute", "AllGather", ALU.bypass,
         replica_groups=[[0, 1, 2, 3], [4, 5, 6, 7]],
         ins=[C.dcin.ap().opt()], outs=[C.dcout.ap().opt()], inc=C.scc)
    C.mark["cc"] = C.scc.v


def _p3_q_g1(C):
    _projection(C, "wq", lambda ec: C.B_q[:, 1024 * ec:1024 * (ec + 1)])
    C.wait("g", C.st, C.mark[("proj_mm", "wq", DC - 1)])
    C.op("g", "dma_start", out=C.G1W[:, :],
         in_=C.din["wg1"].ap().rearrange("p a b -> p (a b)"), inc=C.sgx, incv=16)
    C.barrier("t")
    C.wait("t", C.sgx)
    for dc in range(DC):
        for tt in range(2):
            C.op("t", "matmul", C.PB[:, 512 * tt:512 * (tt + 1)],
                 C.G1W[:, 128 * dc:128 * (dc + 1)],
                 C.B_xn[:, 1024 * dc + 512 * tt:1024 * dc + 512 * (tt + 1)],
                 start=(dc == 0), stop=(dc == DC - 1),
                 inc=(C.st if (tt == 1 and dc == DC - 1) else None))
    C.wait("v", C.st)
    C.op("v", "tensor_copy", C.G1[:, :], C.PB[:, 0:1024], inc=C.sv)
    C.mark["g1"] = C.sv.v


def _p3b_consts2_sstart(C):
    cw = C.C_cw
    C.wait("y", C.st, C.mark[("proj_mm", "wq", 13)])
    C.wait("y", C.scc, C.mark["cc"])
    for jp in range(3):
        if jp >= 1:
            C.wait("y", C.sv, C.mark[("sacc", jp - 1)])
        C.op("y", "dma_start",
             out=C.Mst[:, :].rearrange("p (h e) -> p h e", h=H),
             in_=C.dcout.ap()[2048 * jp:2048 * (jp + 1), :].rearrange(
                 "(h p) e -> p h e", p=128),
             inc=C.sy, incv=16)
        C.mark[("smld", jp)] = C.sy.v
        C.wait("v", C.sy, C.mark[("smld", jp)])
        for h in range(H):
            if jp == 0:
                C.op("v", "tensor_scalar", C.S[:, 128 * h:128 * (h + 1)],
                     C.Mst[:, 128 * h:128 * (h + 1)], cw[:, 736 + h:737 + h],
                     None, ALU.mult, inc=(C.sv if h == H - 1 else None))
            else:
                C.op("v", "tensor_scalar", C.OG[:, 0:128],
                     C.Mst[:, 128 * h:128 * (h + 1)],
                     cw[:, 736 + 16 * jp + h:737 + 16 * jp + h], None, ALU.mult)
                C.op("v", "tensor_add", C.S[:, 128 * h:128 * (h + 1)],
                     C.S[:, 128 * h:128 * (h + 1)], C.OG[:, 0:128],
                     inc=(C.sv if h == H - 1 else None))
        C.mark[("sacc", jp)] = C.sv.v
    C.op("v", "tensor_copy", C.S_bf[:, :], C.S[:, :], inc=C.sv)
    C.mark["sbf"] = C.sv.v
    C.op("v", "tensor_scalar", cw[0:1, 320:336], C.C_ldr[0:1, :], 128.0, None,
         ALU.mult, inc=C.sv)
    C.mark["cdr0"] = C.sv.v
    C.wait("a", C.sv, C.mark["cdr0"])
    C.op("a", "activation", cw[0:1, 592:608], cw[0:1, 320:336], AF.Exp, inc=C.sa)
    C.mark["cdr1"] = C.sa.v
    C.wait("v", C.sa, C.mark["cdr1"])
    C.wait("t", C.sa, C.mark[("proj_ev", "wq", DC - 1)])
    for h in range(H):
        C.op("v", "tensor_scalar", cw[0:1, 336:464], C.C_onesrow[0:1, :],
             cw[0:1, 592 + h:593 + h], None, ALU.mult, inc=C.sv)
        C.mark[("cdr", h)] = C.sv.v
        C.wait("t", C.sv, C.mark[("cdr", h)])
        if h >= 4:
            C.wait("t", C.sv, C.mark[("cdbe", h - 4)])
        C.op("t", "matmul", C.PA[:, 128 * (h % 4):128 * (h % 4 + 1)],
             C.C_onesrow[0:1, :], cw[0:1, 336:464], start=True, stop=True, inc=C.st)
        C.mark[("cdbm", h)] = C.st.v
        C.wait("v", C.st, C.mark[("cdbm", h)])
        C.op("v", "tensor_copy", C.CDB[:, 128 * h:128 * (h + 1)],
             C.PA[:, 128 * (h % 4):128 * (h % 4 + 1)], inc=C.sv)
        C.mark[("cdbe", h)] = C.sv.v


def _a3_attention(C):
    cw = C.C_cw
    C.barrier("t")
    C.mark["pa_free"] = 0
    C.mark["pb_free"] = 0
    C.mark["pc_free"] = 0
    for ch in range(NCH):
        C.wait("t", C.sv, C.mark["pa_free"])
        for et in range(4):
            C.op("t", "matmul", C.PA[:, 512 * et:512 * (et + 1)],
                 C.G1[:, 128 * ch:128 * (ch + 1)],
                 C.C_wg2[:, 512 * et:512 * (et + 1)], start=True, stop=True,
                 inc=(C.st if et == 3 else None))
        C.mark["gate_mm"] = C.st.v
        C.wait("a", C.st, C.mark["gate_mm"])
        C.op("a", "activation", C.SIG[:, :], C.PA[:, :], AF.Sigmoid, inc=C.sa)
        C.mark["sig"] = C.sa.v
        C.wait("t", C.sv, C.mark["pc_free"])
        for h in range(H):
            C.op("t", "transpose", C.PC[:, 128 * h:128 * (h + 1)],
                 C.B_k[:, 1024 * h + 128 * ch:1024 * h + 128 * (ch + 1)],
                 C.C_identb[:, :], inc=(C.st if h == H - 1 else None))
        C.mark["ktr"] = C.st.v
        C.wait("v", C.st, C.mark["ktr"])
        C.op("v", "tensor_mul", C.KTM[:, :], C.PC[:, :], C.KDB[:, :], inc=C.sv)
        C.mark["ktm"] = C.sv.v
        C.wait("t", C.sv, C.mark["pb_free"])
        for w in range(2):
            if w == 1:
                C.wait("t", C.sv, C.mark[("scev", 0)])
            for i in range(8):
                h = 8 * w + i
                C.op("t", "matmul", C.PB[:, 128 * i:128 * (i + 1)],
                     C.B_k[:, 1024 * h + 128 * ch:1024 * h + 128 * (ch + 1)],
                     C.B_q[:, 1024 * h + 128 * ch:1024 * h + 128 * (ch + 1)],
                     start=True, stop=True, inc=(C.st if i == 7 else None))
            C.mark[("sc", w)] = C.st.v
            C.wait("v", C.st, C.mark[("sc", w)])
            C.op("v", "tensor_mul", C.STm[:, 1024 * w:1024 * (w + 1)],
                 C.PB[:, 0:1024], C.DTm[:, 1024 * w:1024 * (w + 1)], inc=C.sv)
            C.mark[("scev", w)] = C.sv.v
        C.op("v", "tensor_mul",
             C.QS[:, :].rearrange("p (h e) -> p h e", h=H),
             C.B_q[:, :].rearrange("p (h t) -> p h t", h=H)[:, :, 128 * ch:128 * (ch + 1)],
             C.QDB[:, :].rearrange("p (h e) -> p h e", h=H), inc=C.sv)
        C.mark["qs"] = C.sv.v
        C.wait("t", C.sv, C.mark["qs"])
        C.wait("t", C.sa, C.mark["sig"])
        for h in range(H):
            C.op("t", "matmul", C.PA[:, 128 * h:128 * (h + 1)],
                 C.STm[:, 128 * h:128 * (h + 1)],
                 C.B_v[:, 2048 * ch + 128 * h:2048 * ch + 128 * (h + 1)],
                 start=True, stop=False)
            C.op("t", "matmul", C.PA[:, 128 * h:128 * (h + 1)],
                 C.QS[:, 128 * h:128 * (h + 1)],
                 C.S_bf[:, 128 * h:128 * (h + 1)],
                 start=False, stop=True, inc=(C.st if h == H - 1 else None))
        C.mark["omm"] = C.st.v
        C.wait("v", C.st, C.mark["omm"])
        C.op("v", "tensor_mul", C.OG[:, :], C.PA[:, :], C.SIG[:, :], inc=C.sv)
        C.mark["og"] = C.sv.v
        if ch < NCH - 1:
            for w in range(2):
                C.wait("t", C.sv, C.mark[("scev", 1)] if w == 0 else C.mark[("sup", 0)])
                for i in range(8):
                    h = 8 * w + i
                    C.op("t", "matmul", C.PB[:, 128 * i:128 * (i + 1)],
                         C.KTM[:, 128 * h:128 * (h + 1)],
                         C.B_v[:, 2048 * ch + 128 * h:2048 * ch + 128 * (h + 1)],
                         start=True, stop=True, inc=(C.st if i == 7 else None))
                C.mark[("kv", w)] = C.st.v
                C.wait("v", C.st, C.mark[("kv", w)])
                C.op("v", "tensor_mul", C.S[:, 1024 * w:1024 * (w + 1)],
                     C.S[:, 1024 * w:1024 * (w + 1)], C.CDB[:, 1024 * w:1024 * (w + 1)])
                C.op("v", "tensor_add", C.S[:, 1024 * w:1024 * (w + 1)],
                     C.S[:, 1024 * w:1024 * (w + 1)], C.PB[:, 0:1024], inc=C.sv)
                C.mark[("sup", w)] = C.sv.v
            C.mark["pb_free"] = C.mark[("sup", 1)]
            C.op("v", "tensor_copy", C.S_bf[:, :], C.S[:, :], inc=C.sv)
            C.mark["sbf"] = C.sv.v
        else:
            C.mark["pb_free"] = C.mark[("scev", 1)]
        C.wait("a", C.sv, C.mark["og"])
        for h in range(H):
            C.op("a", "activation", C.OTM[:, 128 * h:128 * (h + 1)],
                 C.OG[:, 128 * h:128 * (h + 1)], AF.Square,
                 accum_out=cw[:, 864 + h:865 + h],
                 inc=(C.sa if h == H - 1 else None))
        C.mark["sq"] = C.sa.v
        C.mark["pa_free"] = C.mark["og"]
        C.wait("v", C.sa, C.mark["sq"])
        C.op("v", "tensor_scalar", cw[:, 880:896], cw[:, 864:880], 1.0 / 128.0, EPS,
             ALU.mult, ALU.add, inc=C.sv)
        C.mark["rsq"] = C.sv.v
        C.wait("a", C.sv, C.mark["rsq"])
        C.op("a", "activation", cw[:, 880:896], cw[:, 880:896], AF.Sqrt,
             inc=C.sa)
        C.mark["rsqrt"] = C.sa.v
        C.wait("v", C.sa, C.mark["rsqrt"])
        C.op("v", "reciprocal", cw[:, 880:896], cw[:, 880:896])
        for h in range(H):
            C.op("v", "tensor_scalar", C.OTM[:, 128 * h:128 * (h + 1)],
                 C.OG[:, 128 * h:128 * (h + 1)], cw[:, 880 + h:881 + h], None, ALU.mult,
                 inc=(C.sv if h == H - 1 else None))
        C.mark["otm"] = C.sv.v
        C.wait("t", C.sv, C.mark["otm"])
        C.wait("t", C.sv, C.mark["ktm"])
        for h in range(H):
            C.op("t", "transpose", C.PC[:, 128 * h:128 * (h + 1)],
                 C.OTM[:, 128 * h:128 * (h + 1)], C.C_identb[:, :],
                 inc=(C.st if h == H - 1 else None))
        C.mark["otr"] = C.st.v
        C.wait("v", C.st, C.mark["otr"])
        for h in range(H):
            C.op("v", "tensor_scalar",
                 C.B_q[:, 1024 * h + 128 * ch:1024 * h + 128 * (ch + 1)],
                 C.PC[:, 128 * h:128 * (h + 1)], C.C_nw[:, h:h + 1], None, ALU.mult,
                 inc=(C.sv if h == H - 1 else None))
        C.mark["pc_free"] = C.sv.v


def _a4_wo(C):
    C.barrier("tva")
    C.barrier("g")
    NS = len(C.WOSLAB)
    for ec in range(DC):
        if ec >= NS:
            C.wait("g", C.st, C.mark[("wo_mm", ec - NS)])
        C.gslot(ec % NS, C.WOSLAB[ec % NS][:, :], C.din["wo"][ec], ("wo_ld", ec))
        C.gwait("t", ("wo_ld", ec))
        if ec >= 2:
            C.wait("t", C.sv, C.mark[("wo_ev", ec - 2)])
        for dc in range(DC):
            for tt in range(2):
                bank = 1024 * (ec % 2) + 512 * tt
                C.op("t", "matmul", C.PA[:, bank:bank + 512],
                     C.WOSLAB[ec % NS][:, 128 * dc:128 * (dc + 1)],
                     C.B_q[:, 1024 * dc + 512 * tt:1024 * dc + 512 * (tt + 1)],
                     start=(dc == 0), stop=(dc == DC - 1),
                     inc=(C.st if (tt == 1 and dc == DC - 1) else None))
        C.mark[("wo_mm", ec)] = C.st.v
        C.wait("v", C.st, C.mark[("wo_mm", ec)])
        half = 1024 * (ec % 2)
        C.op("v", "tensor_copy", C.B_acc[:, 1024 * ec:1024 * ec + 512],
             C.PA[:, half:half + 512])
        C.op("v", "tensor_copy", C.B_acc[:, 1024 * ec + 512:1024 * (ec + 1)],
             C.PA[:, half + 512:half + 1024], inc=C.sv)
        C.mark[("wo_ev", ec)] = C.sv.v


def _p4_router(C):
    C.barrier("tv")
    r = C.Rsc
    for tc in range(NCH):
        if tc >= 1:
            C.wait("t", C.sv, C.mark[("rt1", tc - 1)])
            C.wait("t", C.sa, C.mark[("rt2", tc - 1)])
        for dc in range(DC):
            C.op("t", "matmul", C.PA[:, 0:8],
                 C.B_xn[:, 1024 * dc + 128 * tc:1024 * dc + 128 * (tc + 1)],
                 C.C_gwt[:, 8 * dc:8 * (dc + 1)],
                 start=(dc == 0), stop=(dc == DC - 1),
                 inc=(C.st if dc == DC - 1 else None))
        C.mark[("lg_mm", tc)] = C.st.v
        C.wait("v", C.st, C.mark[("lg_mm", tc)])
        C.op("v", "reduce_max", r[:, 32:33], C.PA[:, 0:8], axis=AX)
        C.op("v", "tensor_scalar", r[:, 33:34], r[:, 32:33], -1.0, None, ALU.mult,
             inc=C.sv)
        C.mark[("rt1", tc)] = C.sv.v
        C.wait("a", C.sv, C.mark[("rt1", tc)])
        C.op("a", "activation", r[:, 0:8], C.PA[:, 0:8], AF.Exp, bias=r[:, 33:34],
             inc=C.sa)
        C.mark[("rt2", tc)] = C.sa.v
        C.wait("v", C.sa, C.mark[("rt2", tc)])
        C.op("v", "reduce_sum", r[:, 32:33], r[:, 0:8], axis=AX)
        C.op("v", "reciprocal", r[:, 33:34], r[:, 32:33])
        C.op("v", "tensor_scalar", r[:, 0:8], r[:, 0:8], r[:, 33:34], None, ALU.mult)
        C.op("v", "tensor_tensor", r[:, 24:28], r[:, 0:8:2], r[:, 1:8:2], ALU.max)
        C.op("v", "reduce_max", r[:, 32:33], r[:, 24:28], axis=AX)
        C.op("v", "tensor_scalar", r[:, 28:32], r[:, 24:28], r[:, 32:33], None, ALU.is_ge)
        C.op("v", "tensor_scalar", r[:, 28:32], r[:, 28:32], BIG, None, ALU.mult)
        C.op("v", "tensor_sub", r[:, 28:32], r[:, 24:28], r[:, 28:32])
        C.op("v", "reduce_max", r[:, 33:34], r[:, 28:32], axis=AX)
        C.op("v", "tensor_scalar", r[:, 24:28], r[:, 24:28], r[:, 33:34], None, ALU.is_ge)
        C.op("v", "tensor_copy", r[:, 8:16:2], r[:, 24:28])
        C.op("v", "tensor_copy", r[:, 9:16:2], r[:, 24:28])
        C.op("v", "tensor_mul", r[:, 8:16], r[:, 0:8], r[:, 8:16])
        C.op("v", "reduce_max", r[:, 32:33], r[:, 8:16], axis=AX)
        C.op("v", "tensor_scalar", r[:, 16:24], r[:, 8:16], r[:, 32:33], None, ALU.is_ge)
        C.op("v", "tensor_scalar", r[:, 16:24], r[:, 16:24], BIG, None, ALU.mult)
        C.op("v", "tensor_sub", r[:, 16:24], r[:, 8:16], r[:, 16:24])
        C.op("v", "reduce_max", r[:, 33:34], r[:, 16:24], axis=AX)
        C.op("v", "tensor_scalar", r[:, 16:24], r[:, 8:16], r[:, 33:34], None, ALU.is_ge)
        C.op("v", "tensor_mul", C.Rwm[:, 8 * tc:8 * (tc + 1)], r[:, 8:16], r[:, 16:24],
             inc=C.sv)
        C.mark[("wm", tc)] = C.sv.v
    for tc in range(NCH):
        C.wait("t", C.sv, C.mark[("wm", tc)])
        if tc >= 1:
            C.wait("t", C.sv, C.mark[("wms", tc - 1)])
        C.op("t", "transpose", C.PB[0:8, 0:128], C.Rwm[:, 8 * tc:8 * (tc + 1)],
             C.C_identf[:, :], inc=C.st)
        C.mark[("wmt", tc)] = C.st.v
        C.wait("v", C.st, C.mark[("wmt", tc)])
        C.op("v", "tensor_copy", C.WMS[0:8, 128 * tc:128 * (tc + 1)], C.PB[0:8, 0:128],
             inc=C.sv)
        C.mark[("wms", tc)] = C.sv.v
    C.wait("y", C.sv, C.mark[("wms", NCH - 1)])
    C.op("y", "dma_start", out=C.dwms[0:8, :], in_=C.WMS[0:8, :], inc=C.sy, incv=16)
    C.mark["wmsd"] = C.sy.v


def _expert_ffn(C, n):
    C.wait("y", C.sy, C.mark["wmsd"])
    if ("wb_mm",) in C.mark:
        C.wait("y", C.st, C.mark[("wb_mm",)])
    C.op("y", "dma_start", out=C.WMR[0:1, :], in_=C.dwms[n, :][None, :], inc=C.sy, incv=16)
    C.mark["wmr"] = C.sy.v
    C.wait("t", C.sy, C.mark["wmr"])
    C.wait("t", C.sv)
    for tt in range(2):
        C.op("t", "matmul", C.PB[:, 512 * tt:512 * (tt + 1)], C.C_onesrow[0:1, :],
             C.WMR[0:1, 512 * tt:512 * (tt + 1)], start=True, stop=True,
             inc=(C.st if tt == 1 else None))
    C.mark[("wb_mm",)] = C.st.v
    C.wait("v", C.st, C.mark[("wb_mm",)])
    C.op("v", "tensor_copy", C.WB[:, :], C.PB[:, 0:1024], inc=C.sv)
    C.mark["wb"] = C.sv.v

    NSG = len(C.GUS)
    NSD = len(C.DS)
    for fc in range(FC):
        if fc >= NSG:
            C.wait("g", C.st, C.mark[("gu_mm", fc - NSG)])
        elif ("gu_mm", FC - 1) in C.mark:
            C.wait("g", C.st, C.mark[("gu_mm", FC - 1)])
        C.gslot(fc % NSG, C.GUS[fc % NSG][:, :], C.din["guw"][n, fc], ("gu_ld", fc))
        C.gwait("t", ("gu_ld", fc))
        if fc >= 1:
            C.wait("t", C.sv, C.mark[("uss_ev", fc - 1)])
            C.wait("t", C.sa, C.mark[("sg_ev", fc - 1)])
        elif ("uss_ev", FC - 1) in C.mark:
            C.wait("t", C.sv, C.mark[("uss_ev", FC - 1)])
            C.wait("t", C.sa, C.mark[("sg_ev", FC - 1)])
        for dc in range(DC):
            for tt in range(2):
                C.op("t", "matmul", C.PA[:, 512 * tt:512 * (tt + 1)],
                     C.GUS[fc % NSG][:, 128 * dc:128 * (dc + 1)],
                     C.B_xn[:, 1024 * dc + 512 * tt:1024 * dc + 512 * (tt + 1)],
                     start=(dc == 0), stop=(dc == DC - 1))
        for dc in range(DC):
            for tt in range(2):
                C.op("t", "matmul", C.PA[:, 1024 + 512 * tt:1024 + 512 * (tt + 1)],
                     C.GUS[fc % NSG][:, 2048 + 128 * dc:2048 + 128 * (dc + 1)],
                     C.B_xn[:, 1024 * dc + 512 * tt:1024 * dc + 512 * (tt + 1)],
                     start=(dc == 0), stop=(dc == DC - 1),
                     inc=(C.st if (tt == 1 and dc == DC - 1) else None))
        C.mark[("gu_mm", fc)] = C.st.v

        C.wait("a", C.st, C.mark[("gu_mm", fc)])
        if ("a_ev", fc - 2) in C.mark:
            C.wait("a", C.sv, C.mark[("a_ev", fc - 2)])
        C.op("a", "activation", C.SGS[fc % 2][:, :], C.PA[:, 0:1024], AF.Silu, inc=C.sa)
        C.mark[("sg_ev", fc)] = C.sa.v
        C.wait("v", C.st, C.mark[("gu_mm", fc)])
        C.wait("v", C.sv, C.mark["wb"])
        C.op("v", "tensor_mul", C.USS[fc % 2][:, :], C.PA[:, 1024:2048], C.WB[:, :],
             inc=C.sv)
        C.mark[("uss_ev", fc)] = C.sv.v
        C.wait("v", C.sa, C.mark[("sg_ev", fc)])
        C.op("v", "tensor_mul", C.A_E[fc][:, :], C.SGS[fc % 2][:, :], C.USS[fc % 2][:, :],
             inc=C.sv)
        C.mark[("a_ev", fc)] = C.sv.v

    for ec in range(DC):
        if ec >= NSD:
            C.wait("g", C.st, C.mark[("dn_mm", ec - NSD)])
        elif ("dn_mm", DC - 1) in C.mark:
            C.wait("g", C.st, C.mark[("dn_mm", DC - 1)])
        C.gslot(4 + ec % NSD, C.DS[ec % NSD][:, :], C.din["dw"][n, ec], ("dn_ld", ec))
        C.gwait("t", ("dn_ld", ec))
        C.wait("t", C.sv, C.mark[("a_ev", FC - 1)])
        C.wait("t", C.sa, C.mark[("sg_ev", FC - 1)])
        if ec >= 2:
            C.wait("t", C.sv, C.mark[("dn_ev", ec - 2)])
        half = 1024 * (ec % 2)
        for fc in range(FC):
            for tt in range(2):
                C.op("t", "matmul", C.PA[:, half + 512 * tt:half + 512 * (tt + 1)],
                     C.DS[ec % NSD][:, 128 * fc:128 * (fc + 1)],
                     C.A_E[fc][:, 512 * tt:512 * (tt + 1)],
                     start=(fc == 0), stop=(fc == FC - 1),
                     inc=(C.st if (tt == 1 and fc == FC - 1) else None))
        C.mark[("dn_mm", ec)] = C.st.v
        C.wait("v", C.st, C.mark[("dn_mm", ec)])
        for tt in range(2):
            C.op("v", "tensor_add",
                 C.B_acc[:, 1024 * ec + 512 * tt:1024 * ec + 512 * (tt + 1)],
                 C.B_acc[:, 1024 * ec + 512 * tt:1024 * ec + 512 * (tt + 1)],
                 C.PA[:, half + 512 * tt:half + 512 * (tt + 1)],
                 inc=(C.sv if tt == 1 else None))
        C.mark[("dn_ev", ec)] = C.sv.v


def _shared_ffn(C, half, fold_store=False):
    NSG = len(C.GUS)
    NSD = len(C.DS)
    for fc in range(FC):
        if fc >= NSG:
            C.wait("g", C.st, C.mark[("gu_mm", fc - NSG)])
        else:
            C.wait("g", C.st, C.mark[("gu_mm", FC - 1)])
        C.gslot(fc % NSG, C.GUS[fc % NSG][:, :], C.din["sguw"][11 * half + fc],
                ("gu_ld", fc))
        C.gwait("t", ("gu_ld", fc))
        if fc >= 1:
            C.wait("t", C.sv, C.mark[("uss_ev", fc - 1)])
            C.wait("t", C.sa, C.mark[("sg_ev", fc - 1)])
        else:
            C.wait("t", C.sv, C.mark[("uss_ev", FC - 1)])
            C.wait("t", C.sa, C.mark[("sg_ev", FC - 1)])
            C.wait("t", C.sv, C.mark[("dn_ev", DC - 1)])
        for dc in range(DC):
            for tt in range(2):
                C.op("t", "matmul", C.PA[:, 512 * tt:512 * (tt + 1)],
                     C.GUS[fc % NSG][:, 128 * dc:128 * (dc + 1)],
                     C.B_xn[:, 1024 * dc + 512 * tt:1024 * dc + 512 * (tt + 1)],
                     start=(dc == 0), stop=(dc == DC - 1))
        for dc in range(DC):
            for tt in range(2):
                C.op("t", "matmul", C.PA[:, 1024 + 512 * tt:1024 + 512 * (tt + 1)],
                     C.GUS[fc % NSG][:, 2048 + 128 * dc:2048 + 128 * (dc + 1)],
                     C.B_xn[:, 1024 * dc + 512 * tt:1024 * dc + 512 * (tt + 1)],
                     start=(dc == 0), stop=(dc == DC - 1),
                     inc=(C.st if (tt == 1 and dc == DC - 1) else None))
        C.mark[("gu_mm", fc)] = C.st.v

        C.wait("a", C.st, C.mark[("gu_mm", fc)])
        if ("a_ev", fc - 2) in C.mark:
            C.wait("a", C.sv, C.mark[("a_ev", fc - 2)])
        C.op("a", "activation", C.SGS[fc % 2][:, :], C.PA[:, 0:1024], AF.Silu, inc=C.sa)
        C.mark[("sg_ev", fc)] = C.sa.v
        C.wait("v", C.st, C.mark[("gu_mm", fc)])
        C.op("v", "tensor_copy", C.USS[fc % 2][:, :], C.PA[:, 1024:2048], inc=C.sv)
        C.mark[("uss_ev", fc)] = C.sv.v
        C.wait("v", C.sa, C.mark[("sg_ev", fc)])
        C.op("v", "tensor_mul", C.A_E[fc][:, :], C.SGS[fc % 2][:, :], C.USS[fc % 2][:, :],
             inc=C.sv)
        C.mark[("a_ev", fc)] = C.sv.v

    for ec in range(DC):
        if ec >= NSD:
            C.wait("g", C.st, C.mark[("dn_mm", ec - NSD)])
        else:
            C.wait("g", C.st, C.mark[("dn_mm", DC - 1)])
        C.gslot(4 + ec % NSD, C.DS[ec % NSD][:, :], C.din["sdw"][half, ec], ("dn_ld", ec))
        C.gwait("t", ("dn_ld", ec))
        C.wait("t", C.sv, C.mark[("a_ev", FC - 1)])
        C.wait("t", C.sa, C.mark[("sg_ev", FC - 1)])
        if ec >= 2:
            C.wait("t", C.sv, C.mark[("dn_ev", ec - 2)])
        hbank = 1024 * (ec % 2)
        for fc in range(FC):
            for tt in range(2):
                C.op("t", "matmul", C.PA[:, hbank + 512 * tt:hbank + 512 * (tt + 1)],
                     C.DS[ec % NSD][:, 128 * fc:128 * (fc + 1)],
                     C.A_E[fc][:, 512 * tt:512 * (tt + 1)],
                     start=(fc == 0), stop=(fc == FC - 1),
                     inc=(C.st if (tt == 1 and fc == FC - 1) else None))
        C.mark[("dn_mm", ec)] = C.st.v
        C.wait("v", C.st, C.mark[("dn_mm", ec)])
        for tt in range(2):
            C.op("v", "tensor_add",
                 C.B_acc[:, 1024 * ec + 512 * tt:1024 * ec + 512 * (tt + 1)],
                 C.B_acc[:, 1024 * ec + 512 * tt:1024 * ec + 512 * (tt + 1)],
                 C.PA[:, hbank + 512 * tt:hbank + 512 * (tt + 1)],
                 inc=(C.sv if tt == 1 else None))
        C.mark[("dn_ev", ec)] = C.sv.v
        if fold_store:
            if ec >= 2:
                C.wait("y", C.sv, C.mark[("p7add", ec - 2)])
            C.op("y", "dma_start", out=C.XST[ec % 2][:, :], in_=C.din["x"][ec],
                 inc=C.sy, incv=16)
            C.mark[("p7ld", ec)] = C.sy.v
            C.wait("v", C.sy, C.mark[("p7ld", ec)])
            C.op("v", "tensor_add", C.B_acc[:, 1024 * ec:1024 * (ec + 1)],
                 C.B_acc[:, 1024 * ec:1024 * (ec + 1)], C.XST[ec % 2][:, :], inc=C.sv)
            C.mark[("p7add", ec)] = C.sv.v
            C.wait("y", C.sv, C.mark[("p7add", ec)])
            C.op("y", "dma_start", out=C.dout[ec],
                 in_=C.B_acc[:, 1024 * ec:1024 * (ec + 1)], inc=C.sy, incv=16)


def _p7_store(C):
    C.wait("y", C.sy)


def _emit(C):
    C.reset()
    _consts_phase(C)
    _p0_layernorm(C)
    _p1_kv(C)
    _a2_m_collective(C)
    _p3_q_g1(C)
    _p3b_consts2_sstart(C)
    _a3_attention(C)
    _a4_wo(C)
    _p4_router(C)
    C.barrier("tvag")
    for n in range(NE):
        _expert_ffn(C, n)
    _shared_ffn(C, 0)
    _shared_ffn(C, 1, fold_store=True)
    _p7_store(C)


def build_nc():
    nc = bass.Bass("TRN2", target_bir_lowering=False)
    C = K()
    C.nc = nc

    din = {}
    din["x"] = nc.dram_tensor("x", [DC, 128, T], F32, kind="ExternalInput")
    for nm, shp in (
        ("wq", [DC, 128, 2048]), ("wk", [DC, 128, 2048]), ("wv", [DC, 128, 2048]),
        ("wo", [DC, 128, 2048]), ("wg1", [128, DC, 128]), ("wg2", [128, 2048]),
        ("gwt", [128, DC, 8]), ("nw", [128, H]),
        ("ldr", [1, H]), ("ldc", [H, 1]),
        ("guw", [NE, FC, 128, 4096]),
        ("dw", [NE, DC, 128, 1408]),
        ("sguw", [FSC, 128, 4096]),
        ("sdw", [2, DC, 128, 1408]),
        ("identf", [128, 128]), ("difft", [128, 128]), ("maskt", [128, 128]),
        ("onesrow", [1, 128]), ("onescol", [128, 1]), ("cposrow", [1, 128]),
        ("revtrow", [1, 128]), ("cdlrow", [1, 128]), ("sel16", [H, 128]),
        ("scmm", [128, 3]), ("scmk", [128, 3]),
    ):
        din[nm] = nc.dram_tensor(nm, shp, F32, kind="ExternalInput")
    C.din = din
    C.dout = nc.dram_tensor("out", [DC, 128, T], F32, kind="ExternalOutput")
    C.dcin = nc.dram_tensor("cin", [H * 128, 128], F32, kind="Internal")
    C.dcout = nc.dram_tensor("cout", [4 * H * 128, 128], F32, kind="Internal")
    C.dwms = nc.dram_tensor("wmsd", [NE, 1024], F32, kind="Internal")

    A = nc.alloc_sbuf_tensor
    C.B_acc = A("B_acc", [128, 16384], F32)
    C.B_xn = A("B_xn", [128, 16384], BF16)
    C.B_q = A("B_q", [128, 16384], BF16)
    C.B_k = A("B_k", [128, 16384], BF16)
    C.B_v = A("B_v", [128, 16384], BF16)
    C.C_cw = A("C_cw", [128, 896], F32)
    C.C_identf = A("C_identf", [128, 128], F32)
    C.C_identb = A("C_identb", [128, 128], BF16)
    C.C_difft = A("C_difft", [128, 128], F32)
    C.C_maskt = A("C_maskt", [128, 128], F32)
    C.C_wg2 = A("C_wg2", [128, 2048], BF16)
    C.C_gwt = A("C_gwt", [128, DC * 8], BF16)
    C.C_nw = A("C_nw", [128, H], F32)
    C.C_ldr = A("C_ldr", [1, H], F32)
    C.C_ldc = A("C_ldc", [H, 1], F32)
    C.C_sel16 = A("C_sel16", [H, 128], F32)
    C.C_onesrow = A("C_onesrow", [1, 128], F32)
    C.C_onescol = A("C_onescol", [128, 1], F32)
    C.C_cposrow = A("C_cposrow", [1, 128], F32)
    C.C_revtrow = A("C_revtrow", [1, 128], F32)
    C.C_cdlrow = A("C_cdlrow", [1, 128], F32)
    C.C_scmm = A("C_scmm", [128, 3], F32)
    C.C_scmk = A("C_scmk", [128, 3], F32)
    C.Rwm = A("Rwm", [128, 64], F32)
    C.Rsc = A("Rsc", [128, 64], F32)

    AT = nc.alloc_sbuf_tensor_at
    ab = nc.lookup_mloc(C.B_acc).addr
    qb = nc.lookup_mloc(C.B_q).addr
    kb = nc.lookup_mloc(C.B_k).addr
    vb = nc.lookup_mloc(C.B_v).addr
    C.VSall = [AT(f"VSall{i}", [128, 2048], BF16, offset=ab + 4096 * i) for i in range(NCH)]
    C.KTMall = [AT(f"KTMall{i}", [128, 2048], BF16, offset=ab + 32768 + 4096 * i)
                for i in range(NCH)]
    C.WSLAB = [AT(f"WSLAB{i}", [128, 2048], BF16, offset=ab + 4096 * i) for i in range(4)]
    C.VST = [AT(f"VST{i}", [128, 1024], BF16, offset=ab + 16384 + 2048 * i) for i in range(2)]
    C.Mst = AT("Mst", [128, 2048], F32, offset=ab + 0)
    C.CDB = AT("CDB", [128, 2048], F32, offset=ab + 0)
    C.G1W = AT("G1W", [128, 2048], BF16, offset=ab + 8192)
    C.G1 = AT("G1", [128, 1024], BF16, offset=ab + 8192)
    C.QS = AT("QS", [128, 2048], BF16, offset=ab + 12288)
    C.S = AT("S", [128, 2048], F32, offset=ab + 16384)
    C.S_bf = AT("S_bf", [128, 2048], BF16, offset=ab + 24576)
    C.DTm = AT("DTm", [128, 2048], BF16, offset=ab + 28672)
    C.KDB = AT("KDB", [128, 2048], BF16, offset=ab + 32768)
    C.QDB = AT("QDB", [128, 2048], BF16, offset=ab + 36864)
    C.KTM = AT("KTM", [128, 2048], BF16, offset=ab + 40960)
    C.STm = AT("STm", [128, 2048], BF16, offset=ab + 45056)
    C.CDB2 = AT("CDB2", [128, 2048], F32, offset=ab + 53248)
    C.SIG = AT("SIG", [128, 2048], BF16, offset=ab + 49152)
    C.OG = AT("OG", [128, 2048], F32, offset=ab + 53248)
    C.OTM = AT("OTM", [128, 2048], BF16, offset=ab + 61440)
    C.Xsq = [AT(f"Xsq{i}", [128, 1024], F32, offset=qb + 4096 * i) for i in range(2)]
    C.Emean = AT("Emean", [1, 1024], F32, offset=qb + 16384)
    C.Evar = AT("Evar", [1, 1024], F32, offset=qb + 20480)
    C.Emrs = AT("Emrs", [1, 1024], F32, offset=qb + 24576)
    C.Brstd = AT("Brstd", [128, 1024], F32, offset=qb + 0)
    C.Bmrs = AT("Bmrs", [128, 1024], F32, offset=qb + 8192)
    C.A_E = [AT(f"A_E{i}", [128, 1024], BF16, offset=qb + 2048 * i) for i in range(FC)]
    C.GUS = [AT(f"GUS{i}", [128, 4096], BF16, offset=kb + 8192 * i) for i in range(3)]
    C.SGS = [AT(f"SGS{i}", [128, 1024], BF16, offset=kb + 24576 + 2048 * i) for i in range(2)]
    C.USS = [AT(f"USS{i}", [128, 1024], BF16, offset=kb + 28672 + 2048 * i) for i in range(2)]
    C.DS = [AT(f"DS{i}", [128, 1408], BF16, offset=qb + 22528 + 2816 * i) for i in range(3)]
    C.WOSLAB = [AT(f"WOSLAB{i}", [128, 2048], BF16, offset=vb + 4096 * i) for i in range(2)]
    C.WMS = AT("WMS", [8, 1024], F32, offset=vb + 8704)
    C.WB = AT("WB", [128, 1024], F32, offset=vb + 12800)
    C.XST = [AT(f"XST{i}", [128, 1024], F32, offset=vb + 16896 + 4096 * i) for i in range(2)]
    C.WMR = AT("WMR", [1, 1024], F32, offset=vb + 28672)

    C.PA = nc.alloc_psum_tensor("PA", [128, 2048], F32)
    C.PB = nc.alloc_psum_tensor("PB", [128, 1024], F32)
    C.PC = nc.alloc_psum_tensor("PC", [128, 2048], BF16)

    C.sgp = [_Sem(nc.alloc_semaphore(name=f"sgp{i}")) for i in range(7)]
    C.sgx = _Sem(nc.alloc_semaphore(name="sgx"))
    C.st = _Sem(nc.alloc_semaphore(name="st"))
    C.sv = _Sem(nc.alloc_semaphore(name="sv"))
    C.sa = _Sem(nc.alloc_semaphore(name="sa"))
    C.sy = _Sem(nc.alloc_semaphore(name="sy"))
    C.sg = _Sem(nc.alloc_semaphore(name="sg"))
    C.scc = _Sem(nc.alloc_semaphore(name="scc"))

    with nc.Block() as block:
        @block.tensor
        def _(e):
            C.cur, C.h = "t", e
            _emit(C)

        @block.vector
        def _(e):
            C.cur, C.h = "v", e
            _emit(C)

        @block.scalar
        def _(e):
            C.cur, C.h = "a", e
            _emit(C)

        @block.gpsimd
        def _(e):
            C.cur, C.h = "g", e
            _emit(C)

        @block.sync
        def _(e):
            C.cur, C.h = "y", e
            _emit(C)

    _dedupe_ldweights(nc)
    return nc


def _dedupe_ldweights(nc):
    n = 0
    for func in nc.m.functions:
        for blk in func.blocks:
            insts = blk.instructions
            prev_key = None
            for idx, inst in enumerate(insts):
                if isinstance(inst, mybir.InstMatmult) and not inst.is_transpose:
                    key = str(inst.ins[1])
                    if key == prev_key and "float32" not in str(inst.ins[1].dtype):
                        insts[idx] = dataclasses.replace(inst, ldweights=False)
                        n += 1
                    prev_key = key
                else:
                    prev_key = None
    return n



def _lhsT_slabs(W, kc, mc):
    A = W.reshape(kc, 128, mc, 128).transpose(2, 1, 0, 3)
    return np.ascontiguousarray(A).reshape(mc, 128, kc * 128)


def _prep_shared(inputs):
    f = np.float32
    d = {}
    d["wq"] = _lhsT_slabs(np.asarray(inputs["L0_attn_Wq"], f), DC, DC)
    d["wk"] = _lhsT_slabs(np.asarray(inputs["L0_attn_Wk"], f), DC, DC)
    d["wv"] = _lhsT_slabs(np.asarray(inputs["L0_attn_Wv"], f), DC, DC)
    d["wo"] = _lhsT_slabs(np.asarray(inputs["L0_attn_Wo"], f), DC, DC)
    d["wg1"] = np.ascontiguousarray(
        np.asarray(inputs["L0_attn_Wg1"], f).reshape(DC, 128, 128).transpose(1, 0, 2))
    d["wg2"] = np.ascontiguousarray(np.asarray(inputs["L0_attn_Wg2"], f))
    d["gwt"] = np.ascontiguousarray(
        np.asarray(inputs["L0_ffn_gate_weight"], f).T.reshape(DC, 128, NE).transpose(1, 0, 2))
    d["nw"] = np.ascontiguousarray(np.asarray(inputs["L0_attn_norm_w"], f).reshape(H, 128).T)
    ld = np.asarray(inputs["L0_attn_log_decay"], f)
    d["ldr"] = ld.reshape(1, H).copy()
    d["ldc"] = ld.reshape(H, 1).copy()
    gw = np.asarray(inputs["L0_ffn_gate_w"], f)
    uw = np.asarray(inputs["L0_ffn_up_w"], f)
    dw = np.asarray(inputs["L0_ffn_down_w"], f)
    gws = np.stack([_lhsT_slabs(gw[n].T, DC, FC) for n in range(NE)])
    uws = np.stack([_lhsT_slabs(uw[n].T, DC, FC) for n in range(NE)])
    d["guw"] = np.ascontiguousarray(np.concatenate([gws, uws], axis=-1))
    d["dw"] = np.stack([_lhsT_slabs(dw[n].T, FC, DC) for n in range(NE)])
    sgws = _lhsT_slabs(np.asarray(inputs["L0_ffn_sh_gate_w"], f).T, DC, FSC)
    suws = _lhsT_slabs(np.asarray(inputs["L0_ffn_sh_up_w"], f).T, DC, FSC)
    d["sguw"] = np.ascontiguousarray(np.concatenate([sgws, suws], axis=-1))
    sd = np.asarray(inputs["L0_ffn_sh_down_w"], f).T.reshape(FSC, 128, DC, 128)
    sdf = sd.transpose(2, 1, 0, 3)
    d["sdw"] = np.ascontiguousarray(
        np.stack([sdf[:, :, 0:FC, :], sdf[:, :, FC:FSC, :]])).reshape(2, DC, 128, FC * 128)
    idx = np.arange(128, dtype=f)
    d["identf"] = np.eye(128, dtype=f)
    d["difft"] = np.maximum(idx[None, :] - idx[:, None], 0.0).astype(f)
    d["maskt"] = (idx[None, :] >= idx[:, None]).astype(f)
    d["onesrow"] = np.ones((1, 128), f)
    d["onescol"] = np.ones((128, 1), f)
    d["cposrow"] = (idx + 1.0).reshape(1, 128).astype(f)
    d["revtrow"] = (127.0 - idx).reshape(1, 128).astype(f)
    d["cdlrow"] = np.tile(128.0 * (7.0 - np.arange(8, dtype=f)), H).reshape(1, 128)
    sel = np.zeros((H, 128), f)
    for h in range(H):
        sel[h, 8 * h:8 * (h + 1)] = 1.0
    d["sel16"] = sel
    return d


def kernel(**inputs):
    shared = _prep_shared(inputs)
    x = np.asarray(inputs["L0_attn_x"], np.float32).reshape(NCORE * T, D)
    nc = build_nc()
    in_maps = []
    for c in range(NCORE):
        m = dict(shared)
        shard = x[T * c:T * (c + 1)].reshape(T, DC, 128).transpose(1, 2, 0)
        m["x"] = np.ascontiguousarray(shard)
        j = c % 4
        mult = np.array([j - 1 - jp for jp in range(3)], np.float32)
        msk = np.array([1.0 if jp < j else 0.0 for jp in range(3)], np.float32)
        m["scmm"] = np.tile(1024.0 * mult * msk, (128, 1)).astype(np.float32)
        m["scmk"] = np.tile(msk, (128, 1)).astype(np.float32)
        in_maps.append(m)
    res = run_bass_kernel_spmd(nc, in_maps, core_ids=list(range(NCORE)))
    shards = []
    for c in range(NCORE):
        o = res.results[c]["out"]
        shards.append(np.ascontiguousarray(np.transpose(o, (2, 0, 1))).reshape(T, D))
    return np.concatenate(shards, axis=0).reshape(2, 4096, D).astype(np.float32)


if __name__ == "__main__":
    nc = build_nc()
    print("instructions:", nc.n_instructions())
    print("sbuf remaining:", nc.sbuf_bytes_remaining)
